# version 18
# baseline (speedup 1.0000x reference)
import math
import contextlib
import numpy as np
import ml_dtypes

import concourse.bass as bass
import concourse.mybir as mybir
from concourse.bass_utils import run_bass_kernel_spmd

AF = mybir.ActivationFunctionType
ALU = mybir.AluOpType
AX = mybir.AxisListType
F32 = mybir.dt.float32
BF16 = mybir.dt.bfloat16
F16 = mybir.dt.float16
I32 = mybir.dt.int32

NCORES = 8
T = 2048
D = 2048
NDC = 16
H = 16
HD = 128
FI = 5632
NIC = 44
GUP_PC = 11264 // NCORES
DN_PC = D // NCORES
INV_SCALE = 1.0 / math.sqrt(HD)
LN_EPS = 1e-5
RMS_EPS = 1e-6
RG_ALL = [list(range(NCORES))]
RG_PAIR = [[0, 1], [2, 3], [4, 5], [6, 7]]


class Ctx:

    ENGS = ("pe", "act", "dve", "pool", "sp")

    def __init__(self, nc, stack):
        self.nc = nc
        self.stream = {e: [] for e in self.ENGS}
        self.sem_op = {e: stack.enter_context(nc.semaphore(f"so_{e}"))
                       for e in self.ENGS}
        self.sem_dma = {e: stack.enter_context(nc.semaphore(f"sd_{e}"))
                        for e in self.ENGS}
        self.sem_cc = [stack.enter_context(nc.semaphore(f"s_cc{i}"))
                       for i in range(4)]
        self.cnt_cc = 0
        self.cnt_op = {e: 0 for e in self.ENGS}
        self.cnt_dma = {e: 0 for e in self.ENGS}
        self.last_w = {}
        self.readers = {}
        self.clock = {}
        self.n_waits = 0

    def _wait(self, cons, dep):
        eng, kind, cnt = dep
        if cnt <= 0:
            return
        if eng == cons and kind == "op" and eng == "pe":
            return
        key = (cons, eng, kind)
        if self.clock.get(key, 0) >= cnt:
            return
        self.clock[key] = cnt
        if kind == "cc":
            for i in range(eng + 1):
                k2 = (cons, ("cc", i))
                if self.clock.get(k2, 0) < 1:
                    self.clock[k2] = 1
                    sem2 = self.sem_cc[i]
                    self.stream[cons].append(
                        lambda E, sem2=sem2: E.wait_ge(sem2, 1))
                    self.n_waits += 1
            return
        sem = self.sem_op[eng] if kind == "op" else self.sem_dma[eng]
        self.stream[cons].append(lambda E, sem=sem, cnt=cnt: E.wait_ge(sem, cnt))
        self.n_waits += 1

    def _edges(self, eng, reads, writes):
        for k in reads:
            d = self.last_w.get(k)
            if d:
                self._wait(eng, d)
        for k in writes:
            d = self.last_w.get(k)
            if d and not (d[0] == eng and d[1] == "op" and eng != "pe"):
                self._wait(eng, d)
            elif d and d[0] == eng and d[1] == "op" and eng == "pe":
                self._wait(eng, d)
            for r in self.readers.get(k, ()):
                if not (r[0] == eng and r[1] == "op" and eng != "pe"):
                    self._wait(eng, r)

    def _commit(self, tag, reads, writes):
        for k in writes:
            self.last_w[k] = tag
            self.readers[k] = []
        for k in reads:
            self.readers.setdefault(k, []).append(tag)

    def op(self, eng, fn, reads=(), writes=()):
        self._edges(eng, reads, writes)
        self.cnt_op[eng] += 1
        cnt = self.cnt_op[eng]
        sem = self.sem_op[eng]
        self.stream[eng].append(lambda E, fn=fn, sem=sem: fn(E).then_inc(sem, 1))
        self._commit((eng, "op", cnt), reads, writes)

    def dma(self, eng, fn, reads=(), writes=()):
        self._edges(eng, reads, writes)
        self.cnt_dma[eng] += 16
        cnt = self.cnt_dma[eng]
        sem = self.sem_dma[eng]
        self.stream[eng].append(lambda E, fn=fn, sem=sem: fn(E).then_inc(sem, 16))
        self._commit((eng, "dma", cnt), reads, writes)

    def collective(self, fn, reads=(), writes=()):
        self._edges("pool", reads, writes)
        idx = self.cnt_cc
        self.cnt_cc += 1
        sem = self.sem_cc[idx]
        self.stream["pool"].append(
            lambda E, fn=fn, sem=sem: fn(E).then_inc(sem))
        self._commit((idx, "cc", 1), reads, writes)

    def barrier(self):
        so = dict(self.cnt_op)
        sd = dict(self.cnt_dma)
        for cons in self.ENGS:
            for prod in self.ENGS:
                if prod != cons:
                    self._wait(cons, (prod, "op", so[prod]))
                self._wait(cons, (prod, "dma", sd[prod]))
        self.last_w = {}
        self.readers = {}

    def final_waits(self):
        for e in self.ENGS:
            if self.cnt_dma[e]:
                sem, cnt = self.sem_dma[e], self.cnt_dma[e]
                self.stream[e].append(
                    lambda E, sem=sem, cnt=cnt: E.wait_ge(sem, cnt))

    def replay(self, block):
        s = self.stream

        @block.tensor
        def _(E):
            for f in s["pe"]:
                f(E)

        @block.scalar
        def _(E):
            for f in s["act"]:
                f(E)

        @block.vector
        def _(E):
            for f in s["dve"]:
                f(E)

        @block.gpsimd
        def _(E):
            for f in s["pool"]:
                f(E)

        @block.sync
        def _(E):
            for f in s["sp"]:
                f(E)


def build_program(debug=False):
    nc = bass.Bass("TRN2", target_bir_lowering=False, num_devices=NCORES)
    dbg = "ExternalOutput" if debug else "Internal"

    x_tm = nc.dram_tensor("x_tm", [T, D], F32, kind="ExternalInput").ap()
    wkT = nc.dram_tensor("wkT", [D, D], F32, kind="ExternalInput").ap()
    wvT = nc.dram_tensor("wvT", [D, D], F32, kind="ExternalInput").ap()
    woT = nc.dram_tensor("woT", [D, D], F32, kind="ExternalInput").ap()
    wqT_t = nc.dram_tensor("wqT_t", [NDC * D, HD], F32, kind="ExternalInput").ap()
    wuT_t = nc.dram_tensor("wuT_t", [NDC * D, HD], F32, kind="ExternalInput").ap()
    gup_pk = nc.dram_tensor("gup_pk", [GUP_PC, 256], I32, kind="ExternalInput").ap()
    gup_sc = nc.dram_tensor("gup_sc", [GUP_PC, 16], F32, kind="ExternalInput").ap()
    dn_pk = nc.dram_tensor("dn_pk", [DN_PC, 704], I32, kind="ExternalInput").ap()
    dn_sc = nc.dram_tensor("dn_sc", [DN_PC, 44], F32, kind="ExternalInput").ap()
    out = nc.dram_tensor("out", [T, D], F32, kind="ExternalOutput").ap()

    x_bf = nc.dram_tensor("x_bf", [T, D], BF16, kind="Internal").ap()
    x1_f32 = nc.dram_tensor("x1_f32", [T, D], F32, kind=dbg).ap()
    x1_bf = nc.dram_tensor("x1_bf", [T, D], BF16, kind="Internal").ap()
    slice_g = nc.dram_tensor("slice_g", [GUP_PC, D], BF16, kind="Internal").ap()
    slice_d = nc.dram_tensor("slice_d", [DN_PC, FI], BF16, kind="Internal").ap()
    CGR = 11 * D
    CDR = FI * 2
    CROWS = CGR + CDR
    contrib_all = nc.dram_tensor("contrib_all", [CROWS, HD], BF16,
                                 kind="Internal").ap()
    contrib_g = contrib_all[0:CGR, :]
    contrib_d = contrib_all[CGR:CROWS, :].rearrange("(i c) o -> i (c o)", c=2)
    gath_all = nc.dram_tensor("gath_all", [NCORES * CROWS, HD], BF16,
                              kind="Internal").ap()
    kv_part = nc.dram_tensor("kv_part", [HD, D], F32, kind="Internal").ap()
    kv_red = nc.dram_tensor("kv_red", [HD, D], F32, kind="Internal").ap()
    kv_dbg = (nc.dram_tensor("kv_dbg", [HD, D], F32, kind="ExternalOutput").ap()
              if debug else None)

    with contextlib.ExitStack() as st0:
        e0 = st0.enter_context
        ctx = Ctx(nc, st0)

        pb = [e0(nc.psum_tensor(f"pb{i}", [128, 512], F32)) for i in range(8)]
        PB = lambda i: ("pb", i)

        eye_i = e0(nc.sbuf_tensor("eye_i", [128, 128], I32))
        eye_f = e0(nc.sbuf_tensor("eye_f", [128, 128], F32))
        ones_col = e0(nc.sbuf_tensor("ones_col", [128, 1], BF16))
        ones_row = e0(nc.sbuf_tensor("ones_row", [1, 128], BF16))
        kv_bf = e0(nc.sbuf_tensor("kv_bf", [128, D], BF16))
        colt = e0(nc.sbuf_tensor("colt", [128, 8], F32))

        ctx.op("pool", lambda E: E.iota(eye_i[:], pattern=[[1, 128]], base=0,
                                        channel_multiplier=-1),
               writes=["eye_i"])
        ctx.op("dve", lambda E: E.tensor_scalar(eye_f[:], eye_i[:], 0, None,
                                                ALU.is_equal),
               reads=["eye_i"], writes=["eye_f"])
        ctx.op("dve", lambda E: E.memset(ones_col[:], 1.0), writes=["ones_col"])
        ctx.op("dve", lambda E: E.memset(ones_row[:], 1.0), writes=["ones_row"])

        for c in range(4):
            ctx.dma("pool",
                    lambda E, c=c: E.dma_start(x_bf[c * 512:(c + 1) * 512, :],
                                               x_tm[c * 512:(c + 1) * 512, :]),
                    reads=[], writes=[("x_bf", c)])

        with contextlib.ExitStack() as stA:
            eA = stA.enter_context
            kv_acc = eA(nc.sbuf_tensor("kv_acc", [128, D], F32))
            wk_sb = eA(nc.sbuf_tensor("wk_sb", [128, NDC * D], BF16))
            wv_sb = eA(nc.sbuf_tensor("wv_sb", [128, NDC * D], BF16))
            xfm = [eA(nc.sbuf_tensor(f"xfmA{s}", [128, NDC * 512], BF16))
                   for s in range(1)]
            k_sb = eA(nc.sbuf_tensor("k_sb", [128, D], BF16))
            v_sb = eA(nc.sbuf_tensor("v_sb", [128, D], BF16))
            dq_pk = [eA(nc.sbuf_tensor(f"dq_pk{s}", [128, 256], I32))
                     for s in range(2)]
            dq_t0 = eA(nc.sbuf_tensor("dq_t0", [128, 256], BF16))
            dq_sge = eA(nc.sbuf_tensor("dq_sge", [128, 256], BF16))
            dq_m = eA(nc.sbuf_tensor("dq_m", [128, 256], BF16))
            dq_ta = eA(nc.sbuf_tensor("dq_ta", [128, 256], BF16))
            dq_tb = eA(nc.sbuf_tensor("dq_tb", [128, 256], BF16))
            dq_val = eA(nc.sbuf_tensor("dq_val", [128, 2048], BF16))
            dq_out = eA(nc.sbuf_tensor("dq_out", [128, 2048], BF16))
            dq_sc = eA(nc.sbuf_tensor("dq_sc", [128, 264], F32))
            dq_tr = [eA(nc.sbuf_tensor(f"dq_tr{s}", [128, GUP_PC], BF16))
                     for s in range(2)]

            for nm, src, dst in (("wk", wkT, wk_sb), ("wv", wvT, wv_sb)):
                src3 = src.rearrange("(c p) e -> p c e", p=128)
                for c in range(4):
                    ctx.dma("pool",
                            lambda E, c=c, src3=src3, dst=dst:
                            E.dma_start(dst[:, c * 4 * D:(c + 1) * 4 * D],
                                        src3[:, c * 4:(c + 1) * 4, :]),
                            writes=[(nm, c)])

            ctx.dma("sp", lambda E: E.dma_start(
                dq_sc[:, 0:176], gup_sc.rearrange("(c p) g -> p c g", p=128)),
                    writes=["dq_sc"])
            ctx.dma("sp", lambda E: E.dma_start(
                dq_sc[:, 176:264], dn_sc.rearrange("(c p) g -> p c g", p=128)),
                    writes=["dq_sc2"])

            dq_nib = eA(nc.sbuf_tensor("dq_nib", [128, 256], I32))

            def decode2():
                jobs = []
                for oc in range(11):
                    jobs.append(("g", oc, 0, 256, oc * 16, 16, 0))
                for oc in range(2):
                    for hi in range(4):
                        jobs.append(("d", oc, hi * 176, 176,
                                     176 + oc * 44 + hi * 11, 11, hi * 1408))
                for jn, (kind, oc, wo, W, so, NG, co) in enumerate(jobs):
                    slot = jn % 2
                    pk_src = gup_pk if kind == "g" else dn_pk
                    pk3 = pk_src.rearrange("(c p) w -> c p w", p=128)
                    kpk = ("dq_pk", slot)
                    ctx.dma("sp",
                            lambda E, pk3=pk3, oc=oc, wo=wo, W=W, slot=slot:
                            E.dma_start(dq_pk[slot][:, 0:W],
                                        pk3[oc, :, wo:wo + W]),
                            writes=[kpk])
                    yield
                    for k in range(8):
                        ctx.op("dve", lambda E, k=k, W=W, slot=slot:
                               E.tensor_scalar(dq_nib[:, 0:W],
                                               dq_pk[slot][:, 0:W], 4 * k, 15,
                                               ALU.logical_shift_right,
                                               ALU.bitwise_and),
                               reads=[kpk], writes=["nib"])
                        ctx.op("dve", lambda E, W=W: E.tensor_copy(
                            dq_t0[:, 0:W], dq_nib[:, 0:W]),
                               reads=["nib"], writes=["t0"])
                        ctx.op("dve", lambda E, W=W: E.tensor_scalar(
                            dq_sge[:, 0:W], dq_t0[:, 0:W], 8.0, None,
                            ALU.is_ge),
                               reads=["t0"], writes=["sge"])
                        ctx.op("dve", lambda E, W=W: E.scalar_tensor_tensor(
                            dq_m[:, 0:W], dq_sge[:, 0:W], -8.0,
                            dq_t0[:, 0:W], ALU.mult, ALU.add),
                               reads=["t0", "sge"], writes=["m"])
                        ctx.op("dve", lambda E, W=W: E.tensor_scalar(
                            dq_ta[:, 0:W], dq_m[:, 0:W], 0.5, None,
                            ALU.mult),
                               reads=["m"], writes=["ta"])
                        ctx.op("dve", lambda E, W=W: E.scalar_tensor_tensor(
                            dq_ta[:, 0:W], dq_m[:, 0:W], -2.0,
                            dq_ta[:, 0:W], ALU.add, ALU.max),
                               reads=["m", "ta"], writes=["ta"])
                        ctx.op("dve", lambda E, W=W: E.tensor_scalar(
                            dq_tb[:, 0:W], dq_m[:, 0:W], 2.0, -8.0,
                            ALU.mult, ALU.add),
                               reads=["m"], writes=["tb"])
                        ctx.op("dve", lambda E, W=W: E.tensor_max(
                            dq_ta[:, 0:W], dq_ta[:, 0:W], dq_tb[:, 0:W]),
                               reads=["ta", "tb"], writes=["ta"])
                        ctx.op("dve", lambda E, W=W: E.tensor_scalar(
                            dq_tb[:, 0:W], dq_sge[:, 0:W], -2.0, 1.0,
                            ALU.mult, ALU.add),
                               reads=["sge"], writes=["tb"])
                        ctx.op("dve", lambda E, k=k, W=W: E.tensor_mul(
                            dq_val[:, k:8 * W:8], dq_ta[:, 0:W],
                            dq_tb[:, 0:W]),
                               reads=["ta", "tb"], writes=["dq_val"])
                        if k % 3 == 2:
                            yield
                    for g in range(NG):
                        ctx.op("act", lambda E, g=g, so=so: E.activation(
                            dq_out[:, g * 128:(g + 1) * 128],
                            dq_val[:, g * 128:(g + 1) * 128], AF.Copy,
                            scale=dq_sc[:, so + g:so + g + 1]),
                               reads=["dq_val", "dq_sc", "dq_sc2"], writes=["dq_out"])
                    dstt = slice_g if kind == "g" else slice_d
                    W8 = 8 * W
                    ctx.dma("sp", lambda E, dstt=dstt, oc=oc, co=co, W8=W8:
                            E.dma_start(dstt[oc * 128:(oc + 1) * 128,
                                             co:co + W8], dq_out[:, 0:W8]),
                            reads=["dq_out"],
                            writes=[("slice", kind, oc, co)])
                    yield

            dq_gen = decode2()
            dq_done = False

            def pump(n):
                nonlocal dq_done
                for _ in range(n):
                    if dq_done:
                        return
                    try:
                        next(dq_gen)
                    except StopIteration:
                        dq_done = True
                        return

            def proj_group(w_sb, wkeys, slot, lq, eg, bank):
                def f(E, w_sb=w_sb, slot=slot, lq=lq, eg=eg, bank=bank):
                    inst = None
                    for dc in range(NDC):
                        inst = E.matmul(
                            pb[bank][:],
                            xfm[slot][:, dc * 512 + lq * 128:
                                      dc * 512 + (lq + 1) * 128],
                            w_sb[:, dc * D + eg * 512:dc * D + (eg + 1) * 512],
                            start=(dc == 0), stop=(dc == NDC - 1))
                    return inst
                return f

            for lg in range(NDC):
                q, lq = divmod(lg, 4)
                slot = 0
                if lq == 0:
                    for dc in range(NDC):
                        ctx.dma("sp",
                                lambda E, q=q, dc=dc, slot=slot: E.dma_start(
                                    xfm[slot][:, dc * 512:(dc + 1) * 512],
                                    x_bf[q * 512:(q + 1) * 512,
                                         dc * 128:(dc + 1) * 128],
                                    transpose=True),
                                reads=[("x_bf", q)],
                                writes=[("xfmA", slot, dc)])
                xr = [("xfmA", slot, dc) for dc in range(NDC)]
                for eg in range(4):
                    bk, bv = eg % 2, 2 + eg % 2
                    ctx.op("pe", proj_group(wk_sb, None, slot, lq, eg, bk),
                           reads=xr + [("wk", c) for c in range(4)],
                           writes=[PB(bk)])
                    ctx.op("act", lambda E, eg=eg, bk=bk: E.activation(
                        k_sb[:, eg * 512:(eg + 1) * 512], pb[bk][:], AF.Relu,
                        scale=INV_SCALE),
                           reads=[PB(bk)], writes=[("k_sb", eg)])
                    ctx.op("pe", proj_group(wv_sb, None, slot, lq, eg, bv),
                           reads=xr + [("wv", c) for c in range(4)],
                           writes=[PB(bv)])
                    ctx.op("act", lambda E, eg=eg, bv=bv: E.activation(
                        v_sb[:, eg * 512:(eg + 1) * 512], pb[bv][:], AF.Copy),
                           reads=[PB(bv)], writes=[("v_sb", eg)])

                def kvmm(E):
                    inst = None
                    for h in range(H):
                        inst = E.matmul(
                            pb[4 + h // 4][:, (h % 4) * 128:(h % 4 + 1) * 128],
                            k_sb[:, h * 128:(h + 1) * 128],
                            v_sb[:, h * 128:(h + 1) * 128],
                            start=True, stop=True)
                    return inst

                ctx.op("pe", kvmm,
                       reads=[("k_sb", e) for e in range(4)]
                       + [("v_sb", e) for e in range(4)],
                       writes=[PB(4), PB(5), PB(6), PB(7)])
                for b in range(4):
                    if lg == 0:
                        ctx.op("dve", lambda E, b=b: E.tensor_copy(
                            kv_acc[:, b * 512:(b + 1) * 512], pb[4 + b][:]),
                               reads=[PB(4 + b)], writes=[("kv_acc", b)])
                    else:
                        ctx.op("dve", lambda E, b=b: E.tensor_add(
                            kv_acc[:, b * 512:(b + 1) * 512],
                            kv_acc[:, b * 512:(b + 1) * 512], pb[4 + b][:]),
                               reads=[PB(4 + b), ("kv_acc", b)],
                               writes=[("kv_acc", b)])
                pump(4)

            pump(10000)

            cg3 = contrib_g.rearrange("(j d) o -> j d o", j=11)
            g_reads = [("slice", "g", oc, 0) for oc in range(11)]
            for ic in range(NDC):
                s = ic % 2
                ctx.dma("sp", lambda E, ic=ic, s=s: E.dma_start(
                    dq_tr[s][:, 0:GUP_PC],
                    slice_g[:, ic * 128:(ic + 1) * 128], transpose=True),
                        reads=g_reads, writes=[("dq_tr", s)])
                dst = cg3[:, ic * 128:(ic + 1) * 128, :].rearrange(
                    "j p o -> p j o")
                ctx.dma("sp", lambda E, dst=dst, s=s: E.dma_start(
                    dst,
                    dq_tr[s][:, 0:GUP_PC].rearrange("p (j o) -> p j o", o=128)),
                        reads=[("dq_tr", s)], writes=[("contrib_g", ic)])
            d_reads = [("slice", "d", oc, co) for oc in range(2)
                       for co in (0, 2816)]
            for ic in range(NIC):
                s = ic % 2
                ctx.dma("sp", lambda E, ic=ic, s=s: E.dma_start(
                    dq_tr[s][:, 0:DN_PC],
                    slice_d[:, ic * 128:(ic + 1) * 128], transpose=True),
                        reads=d_reads, writes=[("dq_tr", s)])
                ctx.dma("sp", lambda E, ic=ic, s=s: E.dma_start(
                    contrib_d[ic * 128:(ic + 1) * 128, :],
                    dq_tr[s][:, 0:DN_PC]),
                        reads=[("dq_tr", s)], writes=[("contrib_d", ic)])

            ctx.collective(lambda E: E.collective_compute(
                "AllGather", ALU.bypass, replica_groups=RG_ALL,
                ins=[contrib_all.opt()], outs=[gath_all.opt()]),
                reads=[("contrib_g", ic) for ic in range(NDC)]
                + [("contrib_d", ic) for ic in range(NIC)],
                writes=["gath_g", "gath_d"])

            ctx.dma("sp", lambda E: E.dma_start(kv_part, kv_acc[:]),
                    reads=[("kv_acc", b) for b in range(4)],
                    writes=["kv_part"])
            ctx.collective(lambda E: E.collective_compute(
                "AllReduce", ALU.add, replica_groups=RG_PAIR,
                ins=[kv_part.opt()], outs=[kv_red.opt()]),
                reads=["kv_part"], writes=["kv_red"])
            ctx.dma("sp", lambda E: E.dma_start(kv_acc[:], kv_red),
                    reads=["kv_red"], writes=[("kv_acc", "full")])
            if debug:
                ctx.dma("sp", lambda E: E.dma_start(kv_dbg, kv_acc[:]),
                        reads=[("kv_acc", "full")], writes=["kv_dbg"])
            ctx.op("dve", lambda E: E.tensor_copy(kv_bf[:], kv_acc[:]),
                   reads=[("kv_acc", "full")], writes=["kv_bf"])

            ctx.barrier()

        with contextlib.ExitStack() as stB:
            eB = stB.enter_context
            wo_sb = eB(nc.sbuf_tensor("wo_sb", [128, NDC * D], BF16))
            q_sb = eB(nc.sbuf_tensor("q_sb", [128, H * 1024], BF16))
            u_sb = eB(nc.sbuf_tensor("u_sb", [128, H * 1024], BF16))
            obf = [eB(nc.sbuf_tensor(f"obf{p}", [128, 512], BF16))
                   for p in range(2)]
            xfmB = [eB(nc.sbuf_tensor(f"xfmB{s}", [128, NDC * 512], BF16))
                    for s in range(2)]
            wq_blk = [eB(nc.sbuf_tensor(f"wq_blk{s}", [128, D], BF16))
                      for s in range(2)]
            wu_blk = [eB(nc.sbuf_tensor(f"wu_blk{s}", [128, D], BF16))
                      for s in range(2)]
            xres = [eB(nc.sbuf_tensor(f"xres{s}", [128, D], F32))
                    for s in range(2)]
            o2t = [eB(nc.sbuf_tensor(f"o2t{p}", [128, 512], BF16))
                   for p in range(2)]
            srow = [eB(nc.sbuf_tensor(f"srow{p}", [1, 512], F32))
                    for p in range(2)]
            srow_bf = [eB(nc.sbuf_tensor(f"srow_bf{p}", [1, 512], BF16))
                       for p in range(2)]


            wo3 = woT.rearrange("(c p) e -> p c e", p=128)
            for c in range(4):
                ctx.dma("pool",
                        lambda E, c=c: E.dma_start(
                            wo_sb[:, c * 4 * D:(c + 1) * 4 * D],
                            wo3[:, c * 4:(c + 1) * 4, :]),
                        writes=[("wo", c)])

            wq4 = wqT_t.rearrange("(ec dc p) o -> ec p dc o", p=128, dc=NDC)
            wu4 = wuT_t.rearrange("(ec dc p) o -> ec p dc o", p=128, dc=NDC)

            for hf in range(2):
                for ls in range(2):
                    qq = hf * 2 + ls
                    qs = qq % 2
                    for dc in range(NDC):
                        ctx.dma("sp",
                                lambda E, qq=qq, dc=dc, qs=qs: E.dma_start(
                                    xfmB[qs][:, dc * 512:(dc + 1) * 512],
                                    x_bf[qq * 512:(qq + 1) * 512,
                                         dc * 128:(dc + 1) * 128],
                                    transpose=True),
                                reads=[("x_bf", qq)],
                                writes=[("xfmB", qs, dc)])
                    xr = [("xfmB", qs, dc) for dc in range(NDC)]

                    def ldqu(ec):
                        s = ec % 2
                        ctx.dma("pool", lambda E, ec=ec, s=s: E.dma_start(
                            wq_blk[s][:], wq4[ec]),
                                writes=[("wq_blk", s)])
                        ctx.dma("pool", lambda E, ec=ec, s=s: E.dma_start(
                            wu_blk[s][:], wu4[ec]),
                                writes=[("wu_blk", s)])

                    ldqu(0)
                    for ec in range(NDC):
                        s = ec % 2
                        if ec + 1 < NDC:
                            ldqu(ec + 1)

                        def qmm(E, ec, s, bank, w, qs):
                            inst = None
                            for dc in range(NDC):
                                inst = E.matmul(
                                    pb[bank][:],
                                    w[s][:, dc * 128:(dc + 1) * 128],
                                    xfmB[qs][:, dc * 512:(dc + 1) * 512],
                                    start=(dc == 0), stop=(dc == NDC - 1))
                            return inst

                        ctx.op("pe",
                               lambda E, ec=ec, s=s, qs=qs, f=qmm:
                               f(E, ec, s, ec % 2, wq_blk, qs),
                               reads=xr + [("wq_blk", s)],
                               writes=[PB(ec % 2)])
                        ctx.op("act", lambda E, ec=ec, ls=ls: E.activation(
                            q_sb[:, ec * 1024 + ls * 512:
                                 ec * 1024 + (ls + 1) * 512],
                            pb[ec % 2][:], AF.Relu, scale=INV_SCALE),
                               reads=[PB(ec % 2)], writes=[("q_sb", ec, ls)])
                        ctx.op("pe",
                               lambda E, ec=ec, s=s, qs=qs, f=qmm:
                               f(E, ec, s, 2 + ec % 2, wu_blk, qs),
                               reads=xr + [("wu_blk", s)],
                               writes=[PB(2 + ec % 2)])
                        ctx.op("act", lambda E, ec=ec, ls=ls: E.activation(
                            u_sb[:, ec * 1024 + ls * 512:
                                 ec * 1024 + (ls + 1) * 512],
                            pb[2 + ec % 2][:], AF.Silu),
                               reads=[PB(2 + ec % 2)], writes=[("u_sb", ec, ls)])

                for h in range(H):
                    for ls in range(2):
                        p = (h * 2 + ls) % 2
                        ctx.op("pe", lambda E, h=h, ls=ls, p=p: E.matmul(
                            pb[p][:], kv_bf[:, h * 128:(h + 1) * 128],
                            q_sb[:, h * 1024 + ls * 512:
                                 h * 1024 + (ls + 1) * 512],
                            start=True, stop=True),
                               reads=["kv_bf", ("q_sb", h, ls)],
                               writes=[PB(p)])
                        ctx.op("act", lambda E, p=p: E.activation(
                            obf[p][:], pb[p][:], AF.Copy),
                               reads=[PB(p)], writes=[("obf", p)])
                        ctx.op("act", lambda E, p=p: E.activation(
                            o2t[p][:], pb[p][:], AF.Square),
                               reads=[PB(p)], writes=[("o2t", p)])
                        ctx.op("pe", lambda E, p=p: E.matmul(
                            pb[2 + p][0:1, :], ones_col[:], o2t[p][:],
                            start=True, stop=True),
                               reads=["ones_col", ("o2t", p)],
                               writes=[PB(2 + p)])
                        ctx.op("act", lambda E, p=p: E.activation(
                            srow[p][0:1, :], pb[2 + p][0:1, :], AF.Sqrt,
                            scale=1.0 / HD),
                               reads=[PB(2 + p)], writes=[("srow", p)])
                        ctx.op("dve", lambda E, p=p: E.tensor_scalar(
                            srow[p][0:1, :], srow[p][0:1, :], 1e-12, None,
                            ALU.max),
                               reads=[("srow", p)], writes=[("srow", p)])
                        ctx.op("dve", lambda E, p=p: E.reciprocal(
                            srow[p][0:1, :], srow[p][0:1, :]),
                               reads=[("srow", p)], writes=[("srow", p)])
                        ctx.op("dve", lambda E, p=p: E.tensor_copy(
                            srow_bf[p][0:1, :], srow[p][0:1, :]),
                               reads=[("srow", p)], writes=[("srow_bf", p)])
                        ctx.op("pe", lambda E, p=p: E.matmul(
                            pb[4 + p][:], ones_row[:], srow_bf[p][0:1, :],
                            start=True, stop=True),
                               reads=[("srow_bf", p), "ones_row"],
                               writes=[PB(4 + p)])
                        ctx.op("dve", lambda E, p=p: E.tensor_mul(
                            o2t[p][:], obf[p][:], pb[4 + p][:]),
                               reads=[("obf", p), PB(4 + p)],
                               writes=[("o2t", p)])
                        ctx.op("dve", lambda E, h=h, ls=ls, p=p: E.tensor_mul(
                            q_sb[:, h * 1024 + ls * 512:
                                 h * 1024 + (ls + 1) * 512],
                            o2t[p][:],
                            u_sb[:, h * 1024 + ls * 512:
                                 h * 1024 + (ls + 1) * 512]),
                               reads=[("o2t", p), ("u_sb", h, ls)],
                               writes=[("q_sb", h, ls)])

                for lc in range(8):
                    s = lc % 2
                    row0 = hf * 1024 + lc * 128
                    ctx.dma("sp", lambda E, row0=row0, s=s: E.dma_start(
                        xres[s][:], x_tm[row0:row0 + 128, :]),
                            writes=[("xres", s)])
                    ls = lc // 4
                    for eo in range(4):
                        def womm(E, lc=lc, eo=eo, ls=ls):
                            inst = None
                            for ec in range(NDC):
                                inst = E.matmul(
                                    pb[eo][:],
                                    q_sb[:, ec * 1024 + lc * 128:
                                         ec * 1024 + (lc + 1) * 128],
                                    wo_sb[:, ec * D + eo * 512:
                                          ec * D + (eo + 1) * 512],
                                    start=(ec == 0), stop=(ec == NDC - 1))
                            return inst

                        ctx.op("pe", womm,
                               reads=[("q_sb", ec, ls) for ec in range(NDC)]
                               + [("wo", c) for c in range(4)],
                               writes=[PB(eo)])
                        ctx.op("dve", lambda E, eo=eo, s=s: E.tensor_add(
                            xres[s][:, eo * 512:(eo + 1) * 512],
                            xres[s][:, eo * 512:(eo + 1) * 512], pb[eo][:]),
                               reads=[PB(eo), ("xres", s)],
                               writes=[("xres", s)])
                    ctx.op("dve", lambda E, s=s: E.reduce_sum(
                        colt[:, 0:1], xres[s][:], AX.X, negate=True),
                           reads=[("xres", s)], writes=["nm"])
                    ctx.op("dve", lambda E: E.tensor_scalar(
                        colt[:, 0:1], colt[:, 0:1], 1.0 / D, None, ALU.mult),
                           reads=["nm"], writes=["nm"])
                    ctx.op("dve", lambda E, s=s: E.tensor_scalar(
                        xres[s][:], xres[s][:], colt[:, 0:1], None, ALU.add),
                           reads=["nm", ("xres", s)], writes=[("xres", s)])
                    ctx.op("act", lambda E, s=s: E.activation(
                        xres[1 - s][:], xres[s][:], AF.Square,
                        accum_out=colt[:, 1:2]),
                           reads=[("xres", s)],
                           writes=[("xres", 1 - s), "vacc"])
                    ctx.op("dve", lambda E: E.tensor_scalar(
                        colt[:, 2:3], colt[:, 1:2], 1.0 / D, LN_EPS,
                        ALU.mult, ALU.add),
                           reads=["vacc"], writes=["rstd"])
                    ctx.op("act", lambda E: E.activation(
                        colt[:, 2:3], colt[:, 2:3], AF.Sqrt),
                           reads=["rstd"], writes=["rstd"])
                    ctx.op("dve", lambda E: E.reciprocal(
                        colt[:, 2:3], colt[:, 2:3]),
                           reads=["rstd"], writes=["rstd"])
                    ctx.op("dve", lambda E, s=s: E.tensor_scalar(
                        xres[s][:], xres[s][:], colt[:, 2:3], None, ALU.mult),
                           reads=["rstd", ("xres", s)], writes=[("xres", s)])
                    ctx.dma("sp", lambda E, row0=row0, s=s: E.dma_start(
                        x1_f32[row0:row0 + 128, :], xres[s][:]),
                            reads=[("xres", s)], writes=[("x1f", row0)])
                    ctx.dma("pool", lambda E, row0=row0, s=s: E.dma_start(
                        x1_bf[row0:row0 + 128, :], xres[s][:]),
                            reads=[("xres", s)], writes=[("x1b", row0)])

            ctx.barrier()

        with contextlib.ExitStack() as stC:
            eC = stC.enter_context
            h_sb = eC(nc.sbuf_tensor("h_sb", [128, NIC * 512], BF16))
            x1f_sb = [eC(nc.sbuf_tensor(f"x1f{s}", [128, NDC * 512], BF16))
                      for s in range(2)]
            wg_blk = [eC(nc.sbuf_tensor(f"wg_blk{s}", [128, D], BF16))
                      for s in range(4)]
            wd_sb = [eC(nc.sbuf_tensor(f"wd_sb{s}", [128, NIC * 128], BF16))
                     for s in range(2)]
            y_tm = eC(nc.sbuf_tensor("y_tm", [128, 4 * D], F32))
            sq_scr = eC(nc.sbuf_tensor("sq_scrC", [128, D], F32))
            gt = eC(nc.sbuf_tensor("gt", [128, 512], BF16))
            yf = [eC(nc.sbuf_tensor(f"yf{s}", [128, 512], F32))
                  for s in range(2)]
            x1r = [eC(nc.sbuf_tensor(f"x1r{s}", [128, D], F32))
                   for s in range(2)]
            ob = [eC(nc.sbuf_tensor(f"ob{s}", [128, D], F32))
                  for s in range(2)]

            gg4 = {}
            for oc in range(88):
                r, j = divmod(oc, 11)
                blk = gath_all[r * CROWS + j * D * ... if False else
                               r * CROWS + j * 0:0, :]
            gg4 = [gath_all[(oc // 11) * CROWS + (oc % 11) * 2048:
                            (oc // 11) * CROWS + (oc % 11 + 1) * 2048, :]
                   .rearrange("(dc p) o -> p dc o", p=128)
                   for oc in range(88)]

            for t in range(4):
                ts = t % 2
                r0 = t * 512
                for dc in range(NDC):
                    ctx.dma("sp", lambda E, r0=r0, dc=dc, ts=ts: E.dma_start(
                        x1f_sb[ts][:, dc * 512:(dc + 1) * 512],
                        x1_bf[r0:r0 + 512, dc * 128:(dc + 1) * 128],
                        transpose=True),
                            reads=[("x1b", r0), ("x1b", r0 + 128),
                                   ("x1b", r0 + 256), ("x1b", r0 + 384)],
                            writes=[("x1f", ts, dc)])
                xr = [("x1f", ts, dc) for dc in range(NDC)]
                def ldg(ic):
                    sg, su = (ic % 2) * 2, (ic % 2) * 2 + 1
                    ctx.dma("act", lambda E, ic=ic, sg=sg: E.dma_start(
                        wg_blk[sg][:], gg4[ic]),
                            reads=["gath_g"], writes=[("wg", sg)])
                    ctx.dma("act", lambda E, ic=ic, su=su: E.dma_start(
                        wg_blk[su][:], gg4[44 + ic]),
                            reads=["gath_g"], writes=[("wg", su)])

                ldg(0)
                for ic in range(NIC):
                    sg, su = (ic % 2) * 2, (ic % 2) * 2 + 1
                    if ic + 1 < NIC:
                        ldg(ic + 1)

                    def gmm(E, w, bank, ts=ts):
                        inst = None
                        for dc in range(NDC):
                            inst = E.matmul(
                                pb[bank][:], w[:, dc * 128:(dc + 1) * 128],
                                x1f_sb[ts][:, dc * 512:(dc + 1) * 512],
                                start=(dc == 0), stop=(dc == NDC - 1))
                        return inst

                    bkg, bku = ic % 2, 2 + ic % 2
                    ctx.op("pe", lambda E, sg=sg, bkg=bkg, ts=ts:
                           gmm(E, wg_blk[sg], bkg, ts),
                           reads=xr + [("wg", sg)], writes=[PB(bkg)])
                    ctx.op("pe", lambda E, su=su, bku=bku, ts=ts:
                           gmm(E, wg_blk[su], bku, ts),
                           reads=xr + [("wg", su)], writes=[PB(bku)])
                    ctx.op("act", lambda E, bkg=bkg: E.activation(
                        gt[:], pb[bkg][:], AF.Silu),
                           reads=[PB(bkg)], writes=["gt"])
                    ctx.op("dve", lambda E, ic=ic, bku=bku: E.tensor_mul(
                        h_sb[:, ic * 512:(ic + 1) * 512], pb[bku][:], gt[:]),
                           reads=[PB(bku), "gt"], writes=[("h", ic)])
                hr = [("h", ic) for ic in range(NIC)]

                def ldd(oc):
                    r, ch = divmod(oc, 2)
                    s = oc % 2
                    base = r * CROWS + CGR
                    srcd = gath_all[base:base + CDR, :].rearrange(
                        "(i c) o -> i c o", c=2)[:, ch, :].rearrange(
                        "(ic p) o -> p ic o", p=128)
                    ctx.dma("act", lambda E, srcd=srcd, s=s: E.dma_start(
                        wd_sb[s][:], srcd),
                            reads=["gath_d"], writes=[("wd", s)])

                ldd(0)
                for oc in range(NDC):
                    s = oc % 2
                    if oc + 1 < NDC:
                        ldd(oc + 1)

                    def ymm(E, s=s, bank=4 + oc % 2):
                        inst = None
                        for ic in range(NIC):
                            inst = E.matmul(
                                pb[bank][:],
                                wd_sb[s][:, ic * 128:(ic + 1) * 128],
                                h_sb[:, ic * 512:(ic + 1) * 512],
                                start=(ic == 0), stop=(ic == NIC - 1))
                        return inst

                    by = 4 + oc % 2
                    ctx.op("pe", lambda E, s=s, by=by: ymm(E, s, by),
                           reads=hr + [("wd", s)], writes=[PB(by)])
                    ctx.op("act", lambda E, s=s, by=by: E.activation(
                        yf[s][:], pb[by][:], AF.Copy),
                           reads=[PB(by)], writes=[("yf", s)])
                    bt = 6 + oc % 2

                    def ytr(E, s=s, bt=bt):
                        inst = None
                        for qq in range(4):
                            inst = E.transpose(
                                pb[bt][:, qq * 128:(qq + 1) * 128],
                                yf[s][:, qq * 128:(qq + 1) * 128], eye_f[:])
                        return inst

                    ctx.op("pe", ytr, reads=[("yf", s), "eye_f"],
                           writes=[PB(bt)])
                    def yas(E, oc=oc, bt=bt):
                        inst = None
                        for qq in range(4):
                            inst = E.tensor_copy(
                                y_tm[:, qq * D + oc * 128:
                                     qq * D + (oc + 1) * 128],
                                pb[bt][:, qq * 128:(qq + 1) * 128])
                        return inst

                    ctx.op("dve", yas, reads=[PB(bt)], writes=[("y", oc)])
                yr = [("y", oc) for oc in range(NDC)]
                for lc in range(4):
                    s = lc % 2
                    row0 = r0 + lc * 128
                    ctx.dma("sp", lambda E, row0=row0, s=s: E.dma_start(
                        x1r[s][:], x1_f32[row0:row0 + 128, :]),
                            reads=[("x1f", row0)], writes=[("x1r", s)])
                    ctx.op("act", lambda E, lc=lc: E.activation(
                        sq_scr[:], y_tm[:, lc * D:(lc + 1) * D], AF.Square,
                        accum_out=colt[:, 3:4]),
                           reads=yr, writes=["sq_scr", "ssacc"])
                    ctx.op("dve", lambda E: E.tensor_scalar(
                        colt[:, 4:5], colt[:, 3:4], 1.0 / D, RMS_EPS,
                        ALU.mult, ALU.add),
                           reads=["ssacc"], writes=["rs1"])
                    ctx.op("act", lambda E: E.activation(
                        colt[:, 4:5], colt[:, 4:5], AF.Sqrt),
                           reads=["rs1"], writes=["rs1"])
                    ctx.op("dve", lambda E: E.reciprocal(
                        colt[:, 4:5], colt[:, 4:5]),
                           reads=["rs1"], writes=["rs1"])
                    ctx.op("dve", lambda E, lc=lc, s=s: E.scalar_tensor_tensor(
                        ob[s][:], y_tm[:, lc * D:(lc + 1) * D], colt[:, 4:5],
                        x1r[s][:], ALU.mult, ALU.add),
                           reads=yr + ["rs1", ("x1r", s)], writes=[("ob", s)])
                    ctx.op("dve", lambda E, s=s: E.reduce_sum(
                        colt[:, 5:6], ob[s][:], AX.X, negate=True),
                           reads=[("ob", s)], writes=["nm2"])
                    ctx.op("dve", lambda E: E.tensor_scalar(
                        colt[:, 5:6], colt[:, 5:6], 1.0 / D, None, ALU.mult),
                           reads=["nm2"], writes=["nm2"])
                    ctx.op("dve", lambda E, s=s: E.tensor_scalar(
                        ob[s][:], ob[s][:], colt[:, 5:6], None, ALU.add),
                           reads=["nm2", ("ob", s)], writes=[("ob", s)])
                    ctx.op("act", lambda E, s=s: E.activation(
                        sq_scr[:], ob[s][:], AF.Square,
                        accum_out=colt[:, 6:7]),
                           reads=[("ob", s)], writes=["sq_scr", "vacc2"])
                    ctx.op("dve", lambda E: E.tensor_scalar(
                        colt[:, 7:8], colt[:, 6:7], 1.0 / D, LN_EPS,
                        ALU.mult, ALU.add),
                           reads=["vacc2"], writes=["rstd2"])
                    ctx.op("act", lambda E: E.activation(
                        colt[:, 7:8], colt[:, 7:8], AF.Sqrt),
                           reads=["rstd2"], writes=["rstd2"])
                    ctx.op("dve", lambda E: E.reciprocal(
                        colt[:, 7:8], colt[:, 7:8]),
                           reads=["rstd2"], writes=["rstd2"])
                    ctx.op("dve", lambda E, s=s: E.tensor_scalar(
                        ob[s][:], ob[s][:], colt[:, 7:8], None, ALU.mult),
                           reads=["rstd2", ("ob", s)], writes=[("ob", s)])
                    ctx.dma("sp", lambda E, row0=row0, s=s: E.dma_start(
                        out[row0:row0 + 128, :], ob[s][:]),
                            reads=[("ob", s)], writes=[("out", row0)])

        ctx.final_waits()
        ctx.replay(e0(nc.Block()))

    return nc


def _prep_inputs(L0_attn_x, L0_attn_wq, L0_attn_wk, L0_attn_wv, L0_attn_wu,
                 L0_attn_wo, L0_ffn_gate_up_packed, L0_ffn_gate_up_scales,
                 L0_ffn_down_packed, L0_ffn_down_scales):
    xf = np.ascontiguousarray(
        np.asarray(L0_attn_x, dtype=np.float32).reshape(NCORES * T, D))
    wkT = np.ascontiguousarray(np.asarray(L0_attn_wk, np.float32).T)
    wvT = np.ascontiguousarray(np.asarray(L0_attn_wv, np.float32).T)
    woT = np.ascontiguousarray(np.asarray(L0_attn_wo, np.float32).T)
    wqT = np.asarray(L0_attn_wq, np.float32).T
    wuT = np.asarray(L0_attn_wu, np.float32).T
    wqT_t = np.ascontiguousarray(
        wqT.reshape(D, NDC, HD).transpose(1, 0, 2).reshape(NDC * D, HD))
    wuT_t = np.ascontiguousarray(
        wuT.reshape(D, NDC, HD).transpose(1, 0, 2).reshape(NDC * D, HD))
    gup_pk = np.asarray(L0_ffn_gate_up_packed, np.int32)
    gup_scT = np.ascontiguousarray(
        np.asarray(L0_ffn_gate_up_scales).T.astype(np.float32))
    dn_pk = np.asarray(L0_ffn_down_packed, np.int32)
    dn_scT = np.ascontiguousarray(
        np.asarray(L0_ffn_down_scales).T.astype(np.float32))

    in_maps = []
    for c in range(NCORES):
        in_maps.append({
            "x_tm": np.ascontiguousarray(xf[c * T:(c + 1) * T]),
            "wkT": wkT, "wvT": wvT, "woT": woT,
            "wqT_t": wqT_t, "wuT_t": wuT_t,
            "gup_pk": np.ascontiguousarray(gup_pk[c * GUP_PC:(c + 1) * GUP_PC]),
            "gup_sc": np.ascontiguousarray(gup_scT[c * GUP_PC:(c + 1) * GUP_PC]),
            "dn_pk": np.ascontiguousarray(dn_pk[c * DN_PC:(c + 1) * DN_PC]),
            "dn_sc": np.ascontiguousarray(dn_scT[c * DN_PC:(c + 1) * DN_PC]),
        })
    return in_maps


def kernel(**inputs):
    in_maps = _prep_inputs(**inputs)
    nc = build_program()
    res = run_bass_kernel_spmd(nc, in_maps, core_ids=list(range(NCORES)))
    outs = [res.results[c]["out"] for c in range(NCORES)]
    full = np.concatenate(outs, axis=0).reshape(4, 4096, D)
    return np.ascontiguousarray(full.astype(np.float32))


# version 20
# speedup vs baseline: 1.0377x; 1.0377x over previous
import math
import contextlib
import numpy as np
import ml_dtypes

import concourse.bass as bass
import concourse.mybir as mybir
from concourse.bass_utils import run_bass_kernel_spmd

AF = mybir.ActivationFunctionType
ALU = mybir.AluOpType
AX = mybir.AxisListType
F32 = mybir.dt.float32
BF16 = mybir.dt.bfloat16
F16 = mybir.dt.float16
I32 = mybir.dt.int32

NCORES = 8
T = 2048
D = 2048
NDC = 16
H = 16
HD = 128
FI = 5632
NIC = 44
GUP_PC = 11264 // NCORES
DN_PC = D // NCORES
INV_SCALE = 1.0 / math.sqrt(HD)
LN_EPS = 1e-5
RMS_EPS = 1e-6
RG_ALL = [list(range(NCORES))]
RG_PAIR = [[0, 1], [2, 3], [4, 5], [6, 7]]


class Ctx:

    ENGS = ("pe", "act", "dve", "pool", "sp")

    def __init__(self, nc, stack):
        self.nc = nc
        self.stream = {e: [] for e in self.ENGS}
        self.sem_op = {e: stack.enter_context(nc.semaphore(f"so_{e}"))
                       for e in self.ENGS}
        self.sem_dma = {e: stack.enter_context(nc.semaphore(f"sd_{e}"))
                        for e in self.ENGS}
        self.sem_cc = [stack.enter_context(nc.semaphore(f"s_cc{i}"))
                       for i in range(4)]
        self.cnt_cc = 0
        self.cnt_op = {e: 0 for e in self.ENGS}
        self.cnt_dma = {e: 0 for e in self.ENGS}
        self.last_w = {}
        self.readers = {}
        self.clock = {}
        self.n_waits = 0

    def _wait(self, cons, dep):
        eng, kind, cnt = dep
        if cnt <= 0:
            return
        if eng == cons and kind == "op" and eng == "pe":
            return
        key = (cons, eng, kind)
        if self.clock.get(key, 0) >= cnt:
            return
        self.clock[key] = cnt
        if kind == "cc":
            for i in range(eng + 1):
                k2 = (cons, ("cc", i))
                if self.clock.get(k2, 0) < 1:
                    self.clock[k2] = 1
                    sem2 = self.sem_cc[i]
                    self.stream[cons].append(
                        lambda E, sem2=sem2: E.wait_ge(sem2, 1))
                    self.n_waits += 1
            return
        sem = self.sem_op[eng] if kind == "op" else self.sem_dma[eng]
        self.stream[cons].append(lambda E, sem=sem, cnt=cnt: E.wait_ge(sem, cnt))
        self.n_waits += 1

    def _edges(self, eng, reads, writes):
        for k in reads:
            d = self.last_w.get(k)
            if d:
                self._wait(eng, d)
        for k in writes:
            d = self.last_w.get(k)
            if d and not (d[0] == eng and d[1] == "op" and eng != "pe"):
                self._wait(eng, d)
            elif d and d[0] == eng and d[1] == "op" and eng == "pe":
                self._wait(eng, d)
            for r in self.readers.get(k, ()):
                if not (r[0] == eng and r[1] == "op" and eng != "pe"):
                    self._wait(eng, r)

    def _commit(self, tag, reads, writes):
        for k in writes:
            self.last_w[k] = tag
            self.readers[k] = []
        for k in reads:
            self.readers.setdefault(k, []).append(tag)

    def op(self, eng, fn, reads=(), writes=()):
        self._edges(eng, reads, writes)
        self.cnt_op[eng] += 1
        cnt = self.cnt_op[eng]
        sem = self.sem_op[eng]
        self.stream[eng].append(lambda E, fn=fn, sem=sem: fn(E).then_inc(sem, 1))
        self._commit((eng, "op", cnt), reads, writes)

    def dma(self, eng, fn, reads=(), writes=()):
        self._edges(eng, reads, writes)
        self.cnt_dma[eng] += 16
        cnt = self.cnt_dma[eng]
        sem = self.sem_dma[eng]
        self.stream[eng].append(lambda E, fn=fn, sem=sem: fn(E).then_inc(sem, 16))
        self._commit((eng, "dma", cnt), reads, writes)

    def collective(self, fn, reads=(), writes=()):
        self._edges("pool", reads, writes)
        idx = self.cnt_cc
        self.cnt_cc += 1
        sem = self.sem_cc[idx]
        self.stream["pool"].append(
            lambda E, fn=fn, sem=sem: fn(E).then_inc(sem))
        self._commit((idx, "cc", 1), reads, writes)

    def barrier(self):
        so = dict(self.cnt_op)
        sd = dict(self.cnt_dma)
        for cons in self.ENGS:
            for prod in self.ENGS:
                if prod != cons:
                    self._wait(cons, (prod, "op", so[prod]))
                self._wait(cons, (prod, "dma", sd[prod]))
        self.last_w = {}
        self.readers = {}

    def final_waits(self):
        for e in self.ENGS:
            if self.cnt_dma[e]:
                sem, cnt = self.sem_dma[e], self.cnt_dma[e]
                self.stream[e].append(
                    lambda E, sem=sem, cnt=cnt: E.wait_ge(sem, cnt))

    def replay(self, block):
        s = self.stream

        @block.tensor
        def _(E):
            for f in s["pe"]:
                f(E)

        @block.scalar
        def _(E):
            for f in s["act"]:
                f(E)

        @block.vector
        def _(E):
            for f in s["dve"]:
                f(E)

        @block.gpsimd
        def _(E):
            for f in s["pool"]:
                f(E)

        @block.sync
        def _(E):
            for f in s["sp"]:
                f(E)


def build_program(debug=False):
    nc = bass.Bass("TRN2", target_bir_lowering=False, num_devices=NCORES)
    dbg = "ExternalOutput" if debug else "Internal"

    x_tm = nc.dram_tensor("x_tm", [T, D], F32, kind="ExternalInput").ap()
    wkT = nc.dram_tensor("wkT", [D, D], F32, kind="ExternalInput").ap()
    wvT = nc.dram_tensor("wvT", [D, D], F32, kind="ExternalInput").ap()
    woT = nc.dram_tensor("woT", [D, D], F32, kind="ExternalInput").ap()
    wqT_t = nc.dram_tensor("wqT_t", [NDC * D, HD], F32, kind="ExternalInput").ap()
    wuT_t = nc.dram_tensor("wuT_t", [NDC * D, HD], F32, kind="ExternalInput").ap()
    gup_pk = nc.dram_tensor("gup_pk", [GUP_PC, 256], I32, kind="ExternalInput").ap()
    gup_sc = nc.dram_tensor("gup_sc", [GUP_PC, 16], F32, kind="ExternalInput").ap()
    dn_pk = nc.dram_tensor("dn_pk", [DN_PC, 704], I32, kind="ExternalInput").ap()
    dn_sc = nc.dram_tensor("dn_sc", [DN_PC, 44], F32, kind="ExternalInput").ap()
    out = nc.dram_tensor("out", [T, D], F32, kind="ExternalOutput").ap()

    x_bf = nc.dram_tensor("x_bf", [T, D], BF16, kind="Internal").ap()
    x1_f32 = nc.dram_tensor("x1_f32", [T, D], F32, kind=dbg).ap()
    x1_bf = nc.dram_tensor("x1_bf", [T, D], BF16, kind="Internal").ap()
    slice_g = nc.dram_tensor("slice_g", [GUP_PC, D], BF16, kind="Internal").ap()
    slice_d = nc.dram_tensor("slice_d", [DN_PC, FI], BF16, kind="Internal").ap()
    CGR = 11 * D
    CDR = FI * 2
    CROWS = CGR + CDR
    contrib_all = nc.dram_tensor("contrib_all", [CROWS, HD], BF16,
                                 kind="Internal").ap()
    contrib_g = contrib_all[0:CGR, :]
    contrib_d = contrib_all[CGR:CROWS, :].rearrange("(i c) o -> i (c o)", c=2)
    gath_all = nc.dram_tensor("gath_all", [NCORES * CROWS, HD], BF16,
                              kind="Internal").ap()
    kv_part = nc.dram_tensor("kv_part", [HD, D], F32, kind="Internal").ap()
    kv_red = nc.dram_tensor("kv_red", [HD, D], F32, kind="Internal").ap()
    kv_dbg = (nc.dram_tensor("kv_dbg", [HD, D], F32, kind="ExternalOutput").ap()
              if debug else None)

    with contextlib.ExitStack() as st0:
        e0 = st0.enter_context
        ctx = Ctx(nc, st0)

        pb = [e0(nc.psum_tensor(f"pb{i}", [128, 512], F32)) for i in range(8)]
        PB = lambda i: ("pb", i)

        eye_i = e0(nc.sbuf_tensor("eye_i", [128, 128], I32))
        eye_f = e0(nc.sbuf_tensor("eye_f", [128, 128], F32))
        ones_col = e0(nc.sbuf_tensor("ones_col", [128, 1], BF16))
        ones_row = e0(nc.sbuf_tensor("ones_row", [1, 128], BF16))
        kv_bf = e0(nc.sbuf_tensor("kv_bf", [128, D], BF16))
        colt = e0(nc.sbuf_tensor("colt", [128, 8], F32))

        ctx.op("pool", lambda E: E.iota(eye_i[:], pattern=[[1, 128]], base=0,
                                        channel_multiplier=-1),
               writes=["eye_i"])
        ctx.op("dve", lambda E: E.tensor_scalar(eye_f[:], eye_i[:], 0, None,
                                                ALU.is_equal),
               reads=["eye_i"], writes=["eye_f"])
        ctx.op("dve", lambda E: E.memset(ones_col[:], 1.0), writes=["ones_col"])
        ctx.op("dve", lambda E: E.memset(ones_row[:], 1.0), writes=["ones_row"])

        for c in range(4):
            ctx.dma("pool",
                    lambda E, c=c: E.dma_start(x_bf[c * 512:(c + 1) * 512, :],
                                               x_tm[c * 512:(c + 1) * 512, :]),
                    reads=[], writes=[("x_bf", c)])

        with contextlib.ExitStack() as stA:
            eA = stA.enter_context
            kv_acc = eA(nc.sbuf_tensor("kv_acc", [128, D], F32))
            wk_sb = eA(nc.sbuf_tensor("wk_sb", [128, NDC * D], BF16))
            wv_sb = eA(nc.sbuf_tensor("wv_sb", [128, NDC * D], BF16))
            xfm = [eA(nc.sbuf_tensor(f"xfmA{s}", [128, NDC * 512], BF16))
                   for s in range(1)]
            k_sb = eA(nc.sbuf_tensor("k_sb", [128, D], BF16))
            v_sb = eA(nc.sbuf_tensor("v_sb", [128, D], BF16))
            dq_pk = [eA(nc.sbuf_tensor(f"dq_pk{s}", [128, 512], I32))
                     for s in range(2)]
            dq_t0 = eA(nc.sbuf_tensor("dq_t0", [128, 512], BF16))
            dq_sge = eA(nc.sbuf_tensor("dq_sge", [128, 512], BF16))
            dq_m = eA(nc.sbuf_tensor("dq_m", [128, 512], BF16))
            dq_ta = eA(nc.sbuf_tensor("dq_ta", [128, 512], BF16))
            dq_tb = eA(nc.sbuf_tensor("dq_tb", [128, 512], BF16))
            dq_val = eA(nc.sbuf_tensor("dq_val", [128, 4096], BF16))
            dq_out = eA(nc.sbuf_tensor("dq_out", [128, 4096], BF16))
            dq_sc = eA(nc.sbuf_tensor("dq_sc", [128, 264], F32))
            dq_tr = [eA(nc.sbuf_tensor(f"dq_tr{s}", [128, GUP_PC], BF16))
                     for s in range(2)]

            for nm, src, dst in (("wk", wkT, wk_sb), ("wv", wvT, wv_sb)):
                src3 = src.rearrange("(c p) e -> p c e", p=128)
                for c in range(4):
                    ctx.dma("pool",
                            lambda E, c=c, src3=src3, dst=dst:
                            E.dma_start(dst[:, c * 4 * D:(c + 1) * 4 * D],
                                        src3[:, c * 4:(c + 1) * 4, :]),
                            writes=[(nm, c)])

            ctx.dma("sp", lambda E: E.dma_start(
                dq_sc[:, 0:176], gup_sc.rearrange("(c p) g -> p c g", p=128)),
                    writes=["dq_sc"])
            ctx.dma("sp", lambda E: E.dma_start(
                dq_sc[:, 176:264], dn_sc.rearrange("(c p) g -> p c g", p=128)),
                    writes=["dq_sc2"])

            dq_nib = eA(nc.sbuf_tensor("dq_nib", [128, 512], I32))

            def decode2():
                jobs = []
                for j in range(6):
                    parts = [(oc, 0, oc * 16, 16, 0)
                             for oc in range(2 * j, min(2 * j + 2, 11))]
                    jobs.append(("g", 256, parts))
                for oc in range(2):
                    for hi in range(2):
                        parts = [(oc, (2 * hi + q) * 176,
                                  176 + oc * 44 + (2 * hi + q) * 11, 11,
                                  (2 * hi + q) * 1408) for q in range(2)]
                        jobs.append(("d", 176, parts))
                for jn, (kind, W, parts) in enumerate(jobs):
                    NP = len(parts)
                    WT = W * NP
                    slot = jn % 2
                    pk_src = gup_pk if kind == "g" else dn_pk
                    pk3 = pk_src.rearrange("(c p) w -> c p w", p=128)
                    kpk = ("dq_pk", slot)

                    for i, (oc, wo, _so, _ng, _co) in enumerate(parts):
                        ctx.dma("sp", lambda E, i=i, oc=oc, wo=wo, W=W,
                                slot=slot, pk3=pk3: E.dma_start(
                                    dq_pk[slot][:, i * W:(i + 1) * W],
                                    pk3[oc, :, wo:wo + W]),
                                writes=[kpk])
                    yield
                    for k in range(8):
                        ctx.op("dve", lambda E, k=k, W=WT, slot=slot:
                               E.tensor_scalar(dq_nib[:, 0:W],
                                               dq_pk[slot][:, 0:W], 4 * k, 15,
                                               ALU.logical_shift_right,
                                               ALU.bitwise_and),
                               reads=[kpk], writes=["nib"])
                        ctx.op("act", lambda E, W=WT: E.activation(
                            dq_t0[:, 0:W], dq_nib[:, 0:W], AF.Copy),
                               reads=["nib"], writes=["t0"])
                        ctx.op("dve", lambda E, W=WT: E.tensor_scalar(
                            dq_sge[:, 0:W], dq_t0[:, 0:W], 8.0, None,
                            ALU.is_ge),
                               reads=["t0"], writes=["sge"])
                        ctx.op("dve", lambda E, W=WT: E.scalar_tensor_tensor(
                            dq_m[:, 0:W], dq_sge[:, 0:W], -8.0,
                            dq_t0[:, 0:W], ALU.mult, ALU.add),
                               reads=["t0", "sge"], writes=["m"])
                        ctx.op("dve", lambda E, W=WT: E.tensor_scalar(
                            dq_ta[:, 0:W], dq_m[:, 0:W], 0.5, None,
                            ALU.mult),
                               reads=["m"], writes=["ta"])
                        ctx.op("dve", lambda E, W=WT: E.scalar_tensor_tensor(
                            dq_ta[:, 0:W], dq_m[:, 0:W], -2.0,
                            dq_ta[:, 0:W], ALU.add, ALU.max),
                               reads=["m", "ta"], writes=["ta"])
                        ctx.op("dve", lambda E, W=WT: E.tensor_scalar(
                            dq_tb[:, 0:W], dq_m[:, 0:W], 2.0, -8.0,
                            ALU.mult, ALU.add),
                               reads=["m"], writes=["tb"])
                        ctx.op("dve", lambda E, W=WT: E.tensor_max(
                            dq_ta[:, 0:W], dq_ta[:, 0:W], dq_tb[:, 0:W]),
                               reads=["ta", "tb"], writes=["ta"])
                        ctx.op("dve", lambda E, W=WT: E.tensor_scalar(
                            dq_tb[:, 0:W], dq_sge[:, 0:W], -2.0, 1.0,
                            ALU.mult, ALU.add),
                               reads=["sge"], writes=["tb"])
                        ctx.op("dve", lambda E, k=k, W=W, NP=NP:
                               E.tensor_mul(
                                   dq_val[:, k:8 * W * NP:8],
                                   dq_ta[:, 0:W * NP], dq_tb[:, 0:W * NP]),
                               reads=["ta", "tb"], writes=["dq_val"])
                        if k % 3 == 2:
                            yield
                    dstt = slice_g if kind == "g" else slice_d
                    for i, (oc, wo, so, NG, co) in enumerate(parts):
                        for g in range(NG):
                            ctx.op("act", lambda E, i=i, g=g, so=so, W=W:
                                   E.activation(
                                       dq_out[:, i * 8 * W + g * 128:
                                              i * 8 * W + (g + 1) * 128],
                                       dq_val[:, i * 8 * W + g * 128:
                                              i * 8 * W + (g + 1) * 128],
                                       AF.Copy,
                                       scale=dq_sc[:, so + g:so + g + 1]),
                                   reads=["dq_val", "dq_sc", "dq_sc2"],
                                   writes=["dq_out"])
                        ctx.dma("sp", lambda E, dstt=dstt, oc=oc, co=co,
                                i=i, W=W: E.dma_start(
                                    dstt[oc * 128:(oc + 1) * 128,
                                         co:co + 8 * W],
                                    dq_out[:, i * 8 * W:(i + 1) * 8 * W]),
                                reads=["dq_out"],
                                writes=[("slice", kind, oc, co)])
                    yield

            dq_gen = decode2()
            dq_done = False

            def pump(n):
                nonlocal dq_done
                for _ in range(n):
                    if dq_done:
                        return
                    try:
                        next(dq_gen)
                    except StopIteration:
                        dq_done = True
                        return

            def proj_group(w_sb, wkeys, slot, lq, eg, bank):
                def f(E, w_sb=w_sb, slot=slot, lq=lq, eg=eg, bank=bank):
                    inst = None
                    for dc in range(NDC):
                        inst = E.matmul(
                            pb[bank][:],
                            xfm[slot][:, dc * 512 + lq * 128:
                                      dc * 512 + (lq + 1) * 128],
                            w_sb[:, dc * D + eg * 512:dc * D + (eg + 1) * 512],
                            start=(dc == 0), stop=(dc == NDC - 1))
                    return inst
                return f

            for lg in range(NDC):
                q, lq = divmod(lg, 4)
                slot = 0
                if lq == 0:
                    for dc in range(NDC):
                        ctx.dma("sp",
                                lambda E, q=q, dc=dc, slot=slot: E.dma_start(
                                    xfm[slot][:, dc * 512:(dc + 1) * 512],
                                    x_bf[q * 512:(q + 1) * 512,
                                         dc * 128:(dc + 1) * 128],
                                    transpose=True),
                                reads=[("x_bf", q)],
                                writes=[("xfmA", slot, dc)])
                xr = [("xfmA", slot, dc) for dc in range(NDC)]
                for eg in range(4):
                    bk, bv = eg % 2, 2 + eg % 2
                    ctx.op("pe", proj_group(wk_sb, None, slot, lq, eg, bk),
                           reads=xr + [("wk", c) for c in range(4)],
                           writes=[PB(bk)])
                    ctx.op("act", lambda E, eg=eg, bk=bk: E.activation(
                        k_sb[:, eg * 512:(eg + 1) * 512], pb[bk][:], AF.Relu,
                        scale=INV_SCALE),
                           reads=[PB(bk)], writes=[("k_sb", eg)])
                    ctx.op("pe", proj_group(wv_sb, None, slot, lq, eg, bv),
                           reads=xr + [("wv", c) for c in range(4)],
                           writes=[PB(bv)])
                    ctx.op("act", lambda E, eg=eg, bv=bv: E.activation(
                        v_sb[:, eg * 512:(eg + 1) * 512], pb[bv][:], AF.Copy),
                           reads=[PB(bv)], writes=[("v_sb", eg)])

                def kvmm(E):
                    inst = None
                    for h in range(H):
                        inst = E.matmul(
                            pb[4 + h // 4][:, (h % 4) * 128:(h % 4 + 1) * 128],
                            k_sb[:, h * 128:(h + 1) * 128],
                            v_sb[:, h * 128:(h + 1) * 128],
                            start=True, stop=True)
                    return inst

                ctx.op("pe", kvmm,
                       reads=[("k_sb", e) for e in range(4)]
                       + [("v_sb", e) for e in range(4)],
                       writes=[PB(4), PB(5), PB(6), PB(7)])
                for b in range(4):
                    if lg == 0:
                        ctx.op("dve", lambda E, b=b: E.tensor_copy(
                            kv_acc[:, b * 512:(b + 1) * 512], pb[4 + b][:]),
                               reads=[PB(4 + b)], writes=[("kv_acc", b)])
                    else:
                        ctx.op("dve", lambda E, b=b: E.tensor_add(
                            kv_acc[:, b * 512:(b + 1) * 512],
                            kv_acc[:, b * 512:(b + 1) * 512], pb[4 + b][:]),
                               reads=[PB(4 + b), ("kv_acc", b)],
                               writes=[("kv_acc", b)])
                pump(4)

            pump(10000)

            cg3 = contrib_g.rearrange("(j d) o -> j d o", j=11)
            g_reads = [("slice", "g", oc, 0) for oc in range(11)]
            for ic in range(NDC):
                s = ic % 2
                ctx.dma("sp", lambda E, ic=ic, s=s: E.dma_start(
                    dq_tr[s][:, 0:GUP_PC],
                    slice_g[:, ic * 128:(ic + 1) * 128], transpose=True),
                        reads=g_reads, writes=[("dq_tr", s)])
                dst = cg3[:, ic * 128:(ic + 1) * 128, :].rearrange(
                    "j p o -> p j o")
                ctx.dma("sp", lambda E, dst=dst, s=s: E.dma_start(
                    dst,
                    dq_tr[s][:, 0:GUP_PC].rearrange("p (j o) -> p j o", o=128)),
                        reads=[("dq_tr", s)], writes=[("contrib_g", ic)])
            d_reads = [("slice", "d", oc, co) for oc in range(2)
                       for co in (0, 1408, 2816, 4224)]
            for ic in range(NIC):
                s = ic % 2
                ctx.dma("sp", lambda E, ic=ic, s=s: E.dma_start(
                    dq_tr[s][:, 0:DN_PC],
                    slice_d[:, ic * 128:(ic + 1) * 128], transpose=True),
                        reads=d_reads, writes=[("dq_tr", s)])
                ctx.dma("sp", lambda E, ic=ic, s=s: E.dma_start(
                    contrib_d[ic * 128:(ic + 1) * 128, :],
                    dq_tr[s][:, 0:DN_PC]),
                        reads=[("dq_tr", s)], writes=[("contrib_d", ic)])

            ctx.collective(lambda E: E.collective_compute(
                "AllGather", ALU.bypass, replica_groups=RG_ALL,
                ins=[contrib_all.opt()], outs=[gath_all.opt()]),
                reads=[("contrib_g", ic) for ic in range(NDC)]
                + [("contrib_d", ic) for ic in range(NIC)],
                writes=["gath_g", "gath_d"])

            ctx.dma("sp", lambda E: E.dma_start(kv_part, kv_acc[:]),
                    reads=[("kv_acc", b) for b in range(4)],
                    writes=["kv_part"])
            ctx.collective(lambda E: E.collective_compute(
                "AllReduce", ALU.add, replica_groups=RG_PAIR,
                ins=[kv_part.opt()], outs=[kv_red.opt()]),
                reads=["kv_part"], writes=["kv_red"])
            ctx.dma("sp", lambda E: E.dma_start(kv_acc[:], kv_red),
                    reads=["kv_red"], writes=[("kv_acc", "full")])
            if debug:
                ctx.dma("sp", lambda E: E.dma_start(kv_dbg, kv_acc[:]),
                        reads=[("kv_acc", "full")], writes=["kv_dbg"])
            ctx.op("dve", lambda E: E.tensor_copy(kv_bf[:], kv_acc[:]),
                   reads=[("kv_acc", "full")], writes=["kv_bf"])

            ctx.barrier()

        with contextlib.ExitStack() as stB:
            eB = stB.enter_context
            wo_sb = eB(nc.sbuf_tensor("wo_sb", [128, NDC * D], BF16))
            q_sb = eB(nc.sbuf_tensor("q_sb", [128, H * 1024], BF16))
            u_sb = eB(nc.sbuf_tensor("u_sb", [128, H * 1024], BF16))
            obf = [eB(nc.sbuf_tensor(f"obf{p}", [128, 512], BF16))
                   for p in range(2)]
            xfmB = [eB(nc.sbuf_tensor(f"xfmB{s}", [128, NDC * 512], BF16))
                    for s in range(2)]
            wq_blk = [eB(nc.sbuf_tensor(f"wq_blk{s}", [128, D], BF16))
                      for s in range(2)]
            wu_blk = [eB(nc.sbuf_tensor(f"wu_blk{s}", [128, D], BF16))
                      for s in range(2)]
            xres = [eB(nc.sbuf_tensor(f"xres{s}", [128, D], F32))
                    for s in range(2)]
            o2t = [eB(nc.sbuf_tensor(f"o2t{p}", [128, 512], BF16))
                   for p in range(2)]
            srow = [eB(nc.sbuf_tensor(f"srow{p}", [1, 512], F32))
                    for p in range(2)]
            srow_bf = [eB(nc.sbuf_tensor(f"srow_bf{p}", [1, 512], BF16))
                       for p in range(2)]


            wo3 = woT.rearrange("(c p) e -> p c e", p=128)
            for c in range(4):
                ctx.dma("pool",
                        lambda E, c=c: E.dma_start(
                            wo_sb[:, c * 4 * D:(c + 1) * 4 * D],
                            wo3[:, c * 4:(c + 1) * 4, :]),
                        writes=[("wo", c)])

            wq4 = wqT_t.rearrange("(ec dc p) o -> ec p dc o", p=128, dc=NDC)
            wu4 = wuT_t.rearrange("(ec dc p) o -> ec p dc o", p=128, dc=NDC)

            for hf in range(2):
                for ls in range(2):
                    qq = hf * 2 + ls
                    qs = qq % 2
                    for dc in range(NDC):
                        ctx.dma("sp",
                                lambda E, qq=qq, dc=dc, qs=qs: E.dma_start(
                                    xfmB[qs][:, dc * 512:(dc + 1) * 512],
                                    x_bf[qq * 512:(qq + 1) * 512,
                                         dc * 128:(dc + 1) * 128],
                                    transpose=True),
                                reads=[("x_bf", qq)],
                                writes=[("xfmB", qs, dc)])
                    xr = [("xfmB", qs, dc) for dc in range(NDC)]

                    def ldqu(ec):
                        s = ec % 2
                        ctx.dma("pool", lambda E, ec=ec, s=s: E.dma_start(
                            wq_blk[s][:], wq4[ec]),
                                writes=[("wq_blk", s)])
                        ctx.dma("pool", lambda E, ec=ec, s=s: E.dma_start(
                            wu_blk[s][:], wu4[ec]),
                                writes=[("wu_blk", s)])

                    ldqu(0)
                    for ec in range(NDC):
                        s = ec % 2
                        if ec + 1 < NDC:
                            ldqu(ec + 1)

                        def qmm(E, ec, s, bank, w, qs):
                            inst = None
                            for dc in range(NDC):
                                inst = E.matmul(
                                    pb[bank][:],
                                    w[s][:, dc * 128:(dc + 1) * 128],
                                    xfmB[qs][:, dc * 512:(dc + 1) * 512],
                                    start=(dc == 0), stop=(dc == NDC - 1))
                            return inst

                        ctx.op("pe",
                               lambda E, ec=ec, s=s, qs=qs, f=qmm:
                               f(E, ec, s, ec % 2, wq_blk, qs),
                               reads=xr + [("wq_blk", s)],
                               writes=[PB(ec % 2)])
                        ctx.op("act", lambda E, ec=ec, ls=ls: E.activation(
                            q_sb[:, ec * 1024 + ls * 512:
                                 ec * 1024 + (ls + 1) * 512],
                            pb[ec % 2][:], AF.Relu, scale=INV_SCALE),
                               reads=[PB(ec % 2)], writes=[("q_sb", ec, ls)])
                        ctx.op("pe",
                               lambda E, ec=ec, s=s, qs=qs, f=qmm:
                               f(E, ec, s, 2 + ec % 2, wu_blk, qs),
                               reads=xr + [("wu_blk", s)],
                               writes=[PB(2 + ec % 2)])
                        ctx.op("act", lambda E, ec=ec, ls=ls: E.activation(
                            u_sb[:, ec * 1024 + ls * 512:
                                 ec * 1024 + (ls + 1) * 512],
                            pb[2 + ec % 2][:], AF.Silu),
                               reads=[PB(2 + ec % 2)], writes=[("u_sb", ec, ls)])

                for h in range(H):
                    for ls in range(2):
                        p = (h * 2 + ls) % 2
                        ctx.op("pe", lambda E, h=h, ls=ls, p=p: E.matmul(
                            pb[p][:], kv_bf[:, h * 128:(h + 1) * 128],
                            q_sb[:, h * 1024 + ls * 512:
                                 h * 1024 + (ls + 1) * 512],
                            start=True, stop=True),
                               reads=["kv_bf", ("q_sb", h, ls)],
                               writes=[PB(p)])
                        ctx.op("act", lambda E, p=p: E.activation(
                            obf[p][:], pb[p][:], AF.Copy),
                               reads=[PB(p)], writes=[("obf", p)])
                        ctx.op("act", lambda E, p=p: E.activation(
                            o2t[p][:], pb[p][:], AF.Square),
                               reads=[PB(p)], writes=[("o2t", p)])
                        ctx.op("pe", lambda E, p=p: E.matmul(
                            pb[2 + p][0:1, :], ones_col[:], o2t[p][:],
                            start=True, stop=True),
                               reads=["ones_col", ("o2t", p)],
                               writes=[PB(2 + p)])
                        ctx.op("act", lambda E, p=p: E.activation(
                            srow[p][0:1, :], pb[2 + p][0:1, :], AF.Sqrt,
                            scale=1.0 / HD),
                               reads=[PB(2 + p)], writes=[("srow", p)])
                        ctx.op("dve", lambda E, p=p: E.tensor_scalar(
                            srow[p][0:1, :], srow[p][0:1, :], 1e-12, None,
                            ALU.max),
                               reads=[("srow", p)], writes=[("srow", p)])
                        ctx.op("dve", lambda E, p=p: E.reciprocal(
                            srow[p][0:1, :], srow[p][0:1, :]),
                               reads=[("srow", p)], writes=[("srow", p)])
                        ctx.op("dve", lambda E, p=p: E.tensor_copy(
                            srow_bf[p][0:1, :], srow[p][0:1, :]),
                               reads=[("srow", p)], writes=[("srow_bf", p)])
                        ctx.op("pe", lambda E, p=p: E.matmul(
                            pb[4 + p][:], ones_row[:], srow_bf[p][0:1, :],
                            start=True, stop=True),
                               reads=[("srow_bf", p), "ones_row"],
                               writes=[PB(4 + p)])
                        ctx.op("dve", lambda E, p=p: E.tensor_mul(
                            o2t[p][:], obf[p][:], pb[4 + p][:]),
                               reads=[("obf", p), PB(4 + p)],
                               writes=[("o2t", p)])
                        ctx.op("dve", lambda E, h=h, ls=ls, p=p: E.tensor_mul(
                            q_sb[:, h * 1024 + ls * 512:
                                 h * 1024 + (ls + 1) * 512],
                            o2t[p][:],
                            u_sb[:, h * 1024 + ls * 512:
                                 h * 1024 + (ls + 1) * 512]),
                               reads=[("o2t", p), ("u_sb", h, ls)],
                               writes=[("q_sb", h, ls)])

                for lc in range(8):
                    s = lc % 2
                    row0 = hf * 1024 + lc * 128
                    ctx.dma("sp", lambda E, row0=row0, s=s: E.dma_start(
                        xres[s][:], x_tm[row0:row0 + 128, :]),
                            writes=[("xres", s)])
                    ls = lc // 4
                    for eo in range(4):
                        def womm(E, lc=lc, eo=eo, ls=ls):
                            inst = None
                            for ec in range(NDC):
                                inst = E.matmul(
                                    pb[eo][:],
                                    q_sb[:, ec * 1024 + lc * 128:
                                         ec * 1024 + (lc + 1) * 128],
                                    wo_sb[:, ec * D + eo * 512:
                                          ec * D + (eo + 1) * 512],
                                    start=(ec == 0), stop=(ec == NDC - 1))
                            return inst

                        ctx.op("pe", womm,
                               reads=[("q_sb", ec, ls) for ec in range(NDC)]
                               + [("wo", c) for c in range(4)],
                               writes=[PB(eo)])
                        ctx.op("dve", lambda E, eo=eo, s=s: E.tensor_add(
                            xres[s][:, eo * 512:(eo + 1) * 512],
                            xres[s][:, eo * 512:(eo + 1) * 512], pb[eo][:]),
                               reads=[PB(eo), ("xres", s)],
                               writes=[("xres", s)])
                    ctx.op("dve", lambda E, s=s: E.reduce_sum(
                        colt[:, 0:1], xres[s][:], AX.X, negate=True),
                           reads=[("xres", s)], writes=["nm"])
                    ctx.op("dve", lambda E: E.tensor_scalar(
                        colt[:, 0:1], colt[:, 0:1], 1.0 / D, None, ALU.mult),
                           reads=["nm"], writes=["nm"])
                    ctx.op("dve", lambda E, s=s: E.tensor_scalar(
                        xres[s][:], xres[s][:], colt[:, 0:1], None, ALU.add),
                           reads=["nm", ("xres", s)], writes=[("xres", s)])
                    ctx.op("act", lambda E, s=s: E.activation(
                        xres[1 - s][:], xres[s][:], AF.Square,
                        accum_out=colt[:, 1:2]),
                           reads=[("xres", s)],
                           writes=[("xres", 1 - s), "vacc"])
                    ctx.op("dve", lambda E: E.tensor_scalar(
                        colt[:, 2:3], colt[:, 1:2], 1.0 / D, LN_EPS,
                        ALU.mult, ALU.add),
                           reads=["vacc"], writes=["rstd"])
                    ctx.op("act", lambda E: E.activation(
                        colt[:, 2:3], colt[:, 2:3], AF.Sqrt),
                           reads=["rstd"], writes=["rstd"])
                    ctx.op("dve", lambda E: E.reciprocal(
                        colt[:, 2:3], colt[:, 2:3]),
                           reads=["rstd"], writes=["rstd"])
                    ctx.op("dve", lambda E, s=s: E.tensor_scalar(
                        xres[s][:], xres[s][:], colt[:, 2:3], None, ALU.mult),
                           reads=["rstd", ("xres", s)], writes=[("xres", s)])
                    ctx.dma("sp", lambda E, row0=row0, s=s: E.dma_start(
                        x1_f32[row0:row0 + 128, :], xres[s][:]),
                            reads=[("xres", s)], writes=[("x1f", row0)])
                    ctx.dma("pool", lambda E, row0=row0, s=s: E.dma_start(
                        x1_bf[row0:row0 + 128, :], xres[s][:]),
                            reads=[("xres", s)], writes=[("x1b", row0)])

            ctx.barrier()

        with contextlib.ExitStack() as stC:
            eC = stC.enter_context
            h_sb = eC(nc.sbuf_tensor("h_sb", [128, NIC * 512], BF16))
            x1f_sb = [eC(nc.sbuf_tensor(f"x1f{s}", [128, NDC * 512], BF16))
                      for s in range(2)]
            wg_blk = [eC(nc.sbuf_tensor(f"wg_blk{s}", [128, D], BF16))
                      for s in range(4)]
            wd_sb = [eC(nc.sbuf_tensor(f"wd_sb{s}", [128, NIC * 128], BF16))
                     for s in range(2)]
            y_tm = eC(nc.sbuf_tensor("y_tm", [128, 4 * D], F32))
            sq_scr = eC(nc.sbuf_tensor("sq_scrC", [128, D], F32))
            gt = eC(nc.sbuf_tensor("gt", [128, 512], BF16))
            yf = [eC(nc.sbuf_tensor(f"yf{s}", [128, 512], F32))
                  for s in range(2)]
            x1r = [eC(nc.sbuf_tensor(f"x1r{s}", [128, D], F32))
                   for s in range(2)]
            ob = [eC(nc.sbuf_tensor(f"ob{s}", [128, D], F32))
                  for s in range(2)]

            gg4 = {}
            for oc in range(88):
                r, j = divmod(oc, 11)
                blk = gath_all[r * CROWS + j * D * ... if False else
                               r * CROWS + j * 0:0, :]
            gg4 = [gath_all[(oc // 11) * CROWS + (oc % 11) * 2048:
                            (oc // 11) * CROWS + (oc % 11 + 1) * 2048, :]
                   .rearrange("(dc p) o -> p dc o", p=128)
                   for oc in range(88)]

            for t in range(4):
                ts = t % 2
                r0 = t * 512
                for dc in range(NDC):
                    ctx.dma("sp", lambda E, r0=r0, dc=dc, ts=ts: E.dma_start(
                        x1f_sb[ts][:, dc * 512:(dc + 1) * 512],
                        x1_bf[r0:r0 + 512, dc * 128:(dc + 1) * 128],
                        transpose=True),
                            reads=[("x1b", r0), ("x1b", r0 + 128),
                                   ("x1b", r0 + 256), ("x1b", r0 + 384)],
                            writes=[("x1f", ts, dc)])
                xr = [("x1f", ts, dc) for dc in range(NDC)]
                def ldg(ic):
                    sg, su = (ic % 2) * 2, (ic % 2) * 2 + 1
                    ctx.dma("act", lambda E, ic=ic, sg=sg: E.dma_start(
                        wg_blk[sg][:], gg4[ic]),
                            reads=["gath_g"], writes=[("wg", sg)])
                    ctx.dma("act", lambda E, ic=ic, su=su: E.dma_start(
                        wg_blk[su][:], gg4[44 + ic]),
                            reads=["gath_g"], writes=[("wg", su)])

                ldg(0)
                for ic in range(NIC):
                    sg, su = (ic % 2) * 2, (ic % 2) * 2 + 1
                    if ic + 1 < NIC:
                        ldg(ic + 1)

                    def gmm(E, w, bank, ts=ts):
                        inst = None
                        for dc in range(NDC):
                            inst = E.matmul(
                                pb[bank][:], w[:, dc * 128:(dc + 1) * 128],
                                x1f_sb[ts][:, dc * 512:(dc + 1) * 512],
                                start=(dc == 0), stop=(dc == NDC - 1))
                        return inst

                    bkg, bku = ic % 2, 2 + ic % 2
                    ctx.op("pe", lambda E, sg=sg, bkg=bkg, ts=ts:
                           gmm(E, wg_blk[sg], bkg, ts),
                           reads=xr + [("wg", sg)], writes=[PB(bkg)])
                    ctx.op("pe", lambda E, su=su, bku=bku, ts=ts:
                           gmm(E, wg_blk[su], bku, ts),
                           reads=xr + [("wg", su)], writes=[PB(bku)])
                    ctx.op("act", lambda E, bkg=bkg: E.activation(
                        gt[:], pb[bkg][:], AF.Silu),
                           reads=[PB(bkg)], writes=["gt"])
                    ctx.op("dve", lambda E, ic=ic, bku=bku: E.tensor_mul(
                        h_sb[:, ic * 512:(ic + 1) * 512], pb[bku][:], gt[:]),
                           reads=[PB(bku), "gt"], writes=[("h", ic)])
                hr = [("h", ic) for ic in range(NIC)]

                def ldd(oc):
                    r, ch = divmod(oc, 2)
                    s = oc % 2
                    base = r * CROWS + CGR
                    srcd = gath_all[base:base + CDR, :].rearrange(
                        "(i c) o -> i c o", c=2)[:, ch, :].rearrange(
                        "(ic p) o -> p ic o", p=128)
                    ctx.dma("act", lambda E, srcd=srcd, s=s: E.dma_start(
                        wd_sb[s][:], srcd),
                            reads=["gath_d"], writes=[("wd", s)])

                ldd(0)
                for oc in range(NDC):
                    s = oc % 2
                    if oc + 1 < NDC:
                        ldd(oc + 1)

                    def ymm(E, s=s, bank=4 + oc % 2):
                        inst = None
                        for ic in range(NIC):
                            inst = E.matmul(
                                pb[bank][:],
                                wd_sb[s][:, ic * 128:(ic + 1) * 128],
                                h_sb[:, ic * 512:(ic + 1) * 512],
                                start=(ic == 0), stop=(ic == NIC - 1))
                        return inst

                    by = 4 + oc % 2
                    ctx.op("pe", lambda E, s=s, by=by: ymm(E, s, by),
                           reads=hr + [("wd", s)], writes=[PB(by)])
                    ctx.op("act", lambda E, s=s, by=by: E.activation(
                        yf[s][:], pb[by][:], AF.Copy),
                           reads=[PB(by)], writes=[("yf", s)])
                    bt = 6 + oc % 2

                    def ytr(E, s=s, bt=bt):
                        inst = None
                        for qq in range(4):
                            inst = E.transpose(
                                pb[bt][:, qq * 128:(qq + 1) * 128],
                                yf[s][:, qq * 128:(qq + 1) * 128], eye_f[:])
                        return inst

                    ctx.op("pe", ytr, reads=[("yf", s), "eye_f"],
                           writes=[PB(bt)])
                    def yas(E, oc=oc, bt=bt):
                        inst = None
                        for qq in range(4):
                            inst = E.tensor_copy(
                                y_tm[:, qq * D + oc * 128:
                                     qq * D + (oc + 1) * 128],
                                pb[bt][:, qq * 128:(qq + 1) * 128])
                        return inst

                    ctx.op("dve", yas, reads=[PB(bt)], writes=[("y", oc)])
                yr = [("y", oc) for oc in range(NDC)]
                for lc in range(4):
                    s = lc % 2
                    row0 = r0 + lc * 128
                    ctx.dma("sp", lambda E, row0=row0, s=s: E.dma_start(
                        x1r[s][:], x1_f32[row0:row0 + 128, :]),
                            reads=[("x1f", row0)], writes=[("x1r", s)])
                    ctx.op("act", lambda E, lc=lc: E.activation(
                        sq_scr[:], y_tm[:, lc * D:(lc + 1) * D], AF.Square,
                        accum_out=colt[:, 3:4]),
                           reads=yr, writes=["sq_scr", "ssacc"])
                    ctx.op("dve", lambda E: E.tensor_scalar(
                        colt[:, 4:5], colt[:, 3:4], 1.0 / D, RMS_EPS,
                        ALU.mult, ALU.add),
                           reads=["ssacc"], writes=["rs1"])
                    ctx.op("act", lambda E: E.activation(
                        colt[:, 4:5], colt[:, 4:5], AF.Sqrt),
                           reads=["rs1"], writes=["rs1"])
                    ctx.op("dve", lambda E: E.reciprocal(
                        colt[:, 4:5], colt[:, 4:5]),
                           reads=["rs1"], writes=["rs1"])
                    ctx.op("dve", lambda E, lc=lc, s=s: E.scalar_tensor_tensor(
                        ob[s][:], y_tm[:, lc * D:(lc + 1) * D], colt[:, 4:5],
                        x1r[s][:], ALU.mult, ALU.add),
                           reads=yr + ["rs1", ("x1r", s)], writes=[("ob", s)])
                    ctx.op("dve", lambda E, s=s: E.reduce_sum(
                        colt[:, 5:6], ob[s][:], AX.X, negate=True),
                           reads=[("ob", s)], writes=["nm2"])
                    ctx.op("dve", lambda E: E.tensor_scalar(
                        colt[:, 5:6], colt[:, 5:6], 1.0 / D, None, ALU.mult),
                           reads=["nm2"], writes=["nm2"])
                    ctx.op("dve", lambda E, s=s: E.tensor_scalar(
                        ob[s][:], ob[s][:], colt[:, 5:6], None, ALU.add),
                           reads=["nm2", ("ob", s)], writes=[("ob", s)])
                    ctx.op("act", lambda E, s=s: E.activation(
                        sq_scr[:], ob[s][:], AF.Square,
                        accum_out=colt[:, 6:7]),
                           reads=[("ob", s)], writes=["sq_scr", "vacc2"])
                    ctx.op("dve", lambda E: E.tensor_scalar(
                        colt[:, 7:8], colt[:, 6:7], 1.0 / D, LN_EPS,
                        ALU.mult, ALU.add),
                           reads=["vacc2"], writes=["rstd2"])
                    ctx.op("act", lambda E: E.activation(
                        colt[:, 7:8], colt[:, 7:8], AF.Sqrt),
                           reads=["rstd2"], writes=["rstd2"])
                    ctx.op("dve", lambda E: E.reciprocal(
                        colt[:, 7:8], colt[:, 7:8]),
                           reads=["rstd2"], writes=["rstd2"])
                    ctx.op("dve", lambda E, s=s: E.tensor_scalar(
                        ob[s][:], ob[s][:], colt[:, 7:8], None, ALU.mult),
                           reads=["rstd2", ("ob", s)], writes=[("ob", s)])
                    ctx.dma("sp", lambda E, row0=row0, s=s: E.dma_start(
                        out[row0:row0 + 128, :], ob[s][:]),
                            reads=[("ob", s)], writes=[("out", row0)])

        ctx.final_waits()
        ctx.replay(e0(nc.Block()))

    return nc


def _prep_inputs(L0_attn_x, L0_attn_wq, L0_attn_wk, L0_attn_wv, L0_attn_wu,
                 L0_attn_wo, L0_ffn_gate_up_packed, L0_ffn_gate_up_scales,
                 L0_ffn_down_packed, L0_ffn_down_scales):
    xf = np.ascontiguousarray(
        np.asarray(L0_attn_x, dtype=np.float32).reshape(NCORES * T, D))
    wkT = np.ascontiguousarray(np.asarray(L0_attn_wk, np.float32).T)
    wvT = np.ascontiguousarray(np.asarray(L0_attn_wv, np.float32).T)
    woT = np.ascontiguousarray(np.asarray(L0_attn_wo, np.float32).T)
    wqT = np.asarray(L0_attn_wq, np.float32).T
    wuT = np.asarray(L0_attn_wu, np.float32).T
    wqT_t = np.ascontiguousarray(
        wqT.reshape(D, NDC, HD).transpose(1, 0, 2).reshape(NDC * D, HD))
    wuT_t = np.ascontiguousarray(
        wuT.reshape(D, NDC, HD).transpose(1, 0, 2).reshape(NDC * D, HD))
    gup_pk = np.asarray(L0_ffn_gate_up_packed, np.int32)
    gup_scT = np.ascontiguousarray(
        np.asarray(L0_ffn_gate_up_scales).T.astype(np.float32))
    dn_pk = np.asarray(L0_ffn_down_packed, np.int32)
    dn_scT = np.ascontiguousarray(
        np.asarray(L0_ffn_down_scales).T.astype(np.float32))

    in_maps = []
    for c in range(NCORES):
        in_maps.append({
            "x_tm": np.ascontiguousarray(xf[c * T:(c + 1) * T]),
            "wkT": wkT, "wvT": wvT, "woT": woT,
            "wqT_t": wqT_t, "wuT_t": wuT_t,
            "gup_pk": np.ascontiguousarray(gup_pk[c * GUP_PC:(c + 1) * GUP_PC]),
            "gup_sc": np.ascontiguousarray(gup_scT[c * GUP_PC:(c + 1) * GUP_PC]),
            "dn_pk": np.ascontiguousarray(dn_pk[c * DN_PC:(c + 1) * DN_PC]),
            "dn_sc": np.ascontiguousarray(dn_scT[c * DN_PC:(c + 1) * DN_PC]),
        })
    return in_maps


def kernel(**inputs):
    in_maps = _prep_inputs(**inputs)
    nc = build_program()
    res = run_bass_kernel_spmd(nc, in_maps, core_ids=list(range(NCORES)))
    outs = [res.results[c]["out"] for c in range(NCORES)]
    full = np.concatenate(outs, axis=0).reshape(4, 4096, D)
    return np.ascontiguousarray(full.astype(np.float32))


# version 25
# speedup vs baseline: 1.0721x; 1.0332x over previous
import math
import contextlib
import numpy as np
import ml_dtypes

import concourse.bass as bass
import concourse.mybir as mybir
from concourse.bass_utils import run_bass_kernel_spmd

AF = mybir.ActivationFunctionType
ALU = mybir.AluOpType
AX = mybir.AxisListType
F32 = mybir.dt.float32
BF16 = mybir.dt.bfloat16
F16 = mybir.dt.float16
I32 = mybir.dt.int32

NCORES = 8
T = 2048
D = 2048
NDC = 16
H = 16
HD = 128
FI = 5632
NIC = 44
GUP_PC = 11264 // NCORES
DN_PC = D // NCORES
INV_SCALE = 1.0 / math.sqrt(HD)
LN_EPS = 1e-5
RMS_EPS = 1e-6
RG_ALL = [list(range(NCORES))]
RG_PAIR = [[0, 1], [2, 3], [4, 5], [6, 7]]


class Ctx:

    ENGS = ("pe", "act", "dve", "pool", "sp")

    def __init__(self, nc, stack):
        self.nc = nc
        self.stream = {e: [] for e in self.ENGS}
        self.sem_op = {e: stack.enter_context(nc.semaphore(f"so_{e}"))
                       for e in self.ENGS}
        self.sem_dma = {e: stack.enter_context(nc.semaphore(f"sd_{e}"))
                        for e in self.ENGS}
        self.sem_cc = [stack.enter_context(nc.semaphore(f"s_cc{i}"))
                       for i in range(4)]
        self.cnt_cc = 0
        self.cnt_op = {e: 0 for e in self.ENGS}
        self.cnt_dma = {e: 0 for e in self.ENGS}
        self.last_w = {}
        self.readers = {}
        self.clock = {}
        self.n_waits = 0

    def _wait(self, cons, dep):
        eng, kind, cnt = dep
        if cnt <= 0:
            return
        if eng == cons and kind == "op" and eng == "pe":
            return
        key = (cons, eng, kind)
        if self.clock.get(key, 0) >= cnt:
            return
        self.clock[key] = cnt
        if kind == "cc":
            for i in range(eng + 1):
                k2 = (cons, ("cc", i))
                if self.clock.get(k2, 0) < 1:
                    self.clock[k2] = 1
                    sem2 = self.sem_cc[i]
                    self.stream[cons].append(
                        lambda E, sem2=sem2: E.wait_ge(sem2, 1))
                    self.n_waits += 1
            return
        sem = self.sem_op[eng] if kind == "op" else self.sem_dma[eng]
        self.stream[cons].append(lambda E, sem=sem, cnt=cnt: E.wait_ge(sem, cnt))
        self.n_waits += 1

    def _edges(self, eng, reads, writes):
        for k in reads:
            d = self.last_w.get(k)
            if d:
                self._wait(eng, d)
        for k in writes:
            d = self.last_w.get(k)
            if d and not (d[0] == eng and d[1] == "op" and eng != "pe"):
                self._wait(eng, d)
            elif d and d[0] == eng and d[1] == "op" and eng == "pe":
                self._wait(eng, d)
            for r in self.readers.get(k, ()):
                if not (r[0] == eng and r[1] == "op" and eng != "pe"):
                    self._wait(eng, r)

    def _commit(self, tag, reads, writes):
        for k in writes:
            self.last_w[k] = tag
            self.readers[k] = []
        for k in reads:
            self.readers.setdefault(k, []).append(tag)

    def op(self, eng, fn, reads=(), writes=()):
        self._edges(eng, reads, writes)
        self.cnt_op[eng] += 1
        cnt = self.cnt_op[eng]
        sem = self.sem_op[eng]
        self.stream[eng].append(lambda E, fn=fn, sem=sem: fn(E).then_inc(sem, 1))
        self._commit((eng, "op", cnt), reads, writes)

    def dma(self, eng, fn, reads=(), writes=()):
        self._edges(eng, reads, writes)
        self.cnt_dma[eng] += 16
        cnt = self.cnt_dma[eng]
        sem = self.sem_dma[eng]
        self.stream[eng].append(lambda E, fn=fn, sem=sem: fn(E).then_inc(sem, 16))
        self._commit((eng, "dma", cnt), reads, writes)

    def collective(self, fn, reads=(), writes=()):
        self._edges("pool", reads, writes)
        idx = self.cnt_cc
        self.cnt_cc += 1
        sem = self.sem_cc[idx]
        self.stream["pool"].append(
            lambda E, fn=fn, sem=sem: fn(E).then_inc(sem))
        self._commit((idx, "cc", 1), reads, writes)

    def barrier(self):
        so = dict(self.cnt_op)
        sd = dict(self.cnt_dma)
        for cons in self.ENGS:
            for prod in self.ENGS:
                if prod != cons:
                    self._wait(cons, (prod, "op", so[prod]))
                self._wait(cons, (prod, "dma", sd[prod]))
        self.last_w = {}
        self.readers = {}

    def final_waits(self):
        for e in self.ENGS:
            if self.cnt_dma[e]:
                sem, cnt = self.sem_dma[e], self.cnt_dma[e]
                self.stream[e].append(
                    lambda E, sem=sem, cnt=cnt: E.wait_ge(sem, cnt))

    def replay(self, block):
        s = self.stream

        @block.tensor
        def _(E):
            for f in s["pe"]:
                f(E)

        @block.scalar
        def _(E):
            for f in s["act"]:
                f(E)

        @block.vector
        def _(E):
            for f in s["dve"]:
                f(E)

        @block.gpsimd
        def _(E):
            for f in s["pool"]:
                f(E)

        @block.sync
        def _(E):
            for f in s["sp"]:
                f(E)


def build_program(debug=False):
    nc = bass.Bass("TRN2", target_bir_lowering=False, num_devices=NCORES)
    dbg = "ExternalOutput" if debug else "Internal"

    x_tm = nc.dram_tensor("x_tm", [T, D], F32, kind="ExternalInput").ap()
    wkT = nc.dram_tensor("wkT", [D, D], F32, kind="ExternalInput").ap()
    wvT = nc.dram_tensor("wvT", [D, D], F32, kind="ExternalInput").ap()
    woT = nc.dram_tensor("woT", [D, D], F32, kind="ExternalInput").ap()
    wqT_t = nc.dram_tensor("wqT_t", [NDC * D, HD], F32, kind="ExternalInput").ap()
    wuT_t = nc.dram_tensor("wuT_t", [NDC * D, HD], F32, kind="ExternalInput").ap()
    gup_pk = nc.dram_tensor("gup_pk", [GUP_PC, 256], I32, kind="ExternalInput").ap()
    gup_sc = nc.dram_tensor("gup_sc", [GUP_PC, 16], F32, kind="ExternalInput").ap()
    dn_pk = nc.dram_tensor("dn_pk", [DN_PC, 704], I32, kind="ExternalInput").ap()
    dn_sc = nc.dram_tensor("dn_sc", [DN_PC, 44], F32, kind="ExternalInput").ap()
    out = nc.dram_tensor("out", [T, D], F32, kind="ExternalOutput").ap()

    x_bf = nc.dram_tensor("x_bf", [T, D], BF16, kind="Internal").ap()
    wq_bf = nc.dram_tensor("wq_bf", [NDC * D, HD], BF16, kind="Internal").ap()
    wu_bf = nc.dram_tensor("wu_bf", [NDC * D, HD], BF16, kind="Internal").ap()
    x1_f32 = nc.dram_tensor("x1_f32", [T, D], F32, kind=dbg).ap()
    x1_bf = nc.dram_tensor("x1_bf", [T, D], BF16, kind="Internal").ap()
    slice_g = nc.dram_tensor("slice_g", [GUP_PC, D], BF16, kind="Internal").ap()
    slice_d = nc.dram_tensor("slice_d", [DN_PC, FI], BF16, kind="Internal").ap()
    CGR = 11 * D
    CDR = FI * 2
    CROWS = CGR + CDR
    contrib_all = nc.dram_tensor("contrib_all", [CROWS, HD], BF16,
                                 kind="Internal").ap()
    contrib_g = contrib_all[0:CGR, :]
    contrib_d = contrib_all[CGR:CROWS, :].rearrange("(i c) o -> i (c o)", c=2)
    gath_all = nc.dram_tensor("gath_all", [NCORES * CROWS, HD], BF16,
                              kind="Internal").ap()
    kv_part = nc.dram_tensor("kv_part", [HD, D], F32, kind="Internal").ap()
    kv_red = nc.dram_tensor("kv_red", [HD, D], F32, kind="Internal").ap()
    kv_dbg = (nc.dram_tensor("kv_dbg", [HD, D], F32, kind="ExternalOutput").ap()
              if debug else None)

    with contextlib.ExitStack() as st0:
        e0 = st0.enter_context
        ctx = Ctx(nc, st0)

        pb = [e0(nc.psum_tensor(f"pb{i}", [128, 512], F32)) for i in range(8)]
        PB = lambda i: ("pb", i)

        eye_i = e0(nc.sbuf_tensor("eye_i", [128, 128], I32))
        eye_f = e0(nc.sbuf_tensor("eye_f", [128, 128], F32))
        ones_col = e0(nc.sbuf_tensor("ones_col", [128, 1], BF16))
        ones_row = e0(nc.sbuf_tensor("ones_row", [1, 128], BF16))
        kv_bf = e0(nc.sbuf_tensor("kv_bf", [128, D], BF16))
        colt = e0(nc.sbuf_tensor("colt", [128, 8], F32))

        ctx.op("pool", lambda E: E.iota(eye_i[:], pattern=[[1, 128]], base=0,
                                        channel_multiplier=-1),
               writes=["eye_i"])
        ctx.op("dve", lambda E: E.tensor_scalar(eye_f[:], eye_i[:], 0, None,
                                                ALU.is_equal),
               reads=["eye_i"], writes=["eye_f"])
        ctx.op("dve", lambda E: E.memset(ones_col[:], 1.0), writes=["ones_col"])
        ctx.op("dve", lambda E: E.memset(ones_row[:], 1.0), writes=["ones_row"])

        for c in range(4):
            ctx.dma("pool",
                    lambda E, c=c: E.dma_start(x_bf[c * 512:(c + 1) * 512, :],
                                               x_tm[c * 512:(c + 1) * 512, :]),
                    reads=[], writes=[("x_bf", c)])
        for c in range(4):
            R = NDC * D // 4
            ctx.dma("pool", lambda E, c=c, R=R: E.dma_start(
                wq_bf[c * R:(c + 1) * R, :], wqT_t[c * R:(c + 1) * R, :]),
                    writes=[("wqbf", c)])
            ctx.dma("pool", lambda E, c=c, R=R: E.dma_start(
                wu_bf[c * R:(c + 1) * R, :], wuT_t[c * R:(c + 1) * R, :]),
                    writes=[("wubf", c)])

        with contextlib.ExitStack() as stA:
            eA = stA.enter_context
            kv_acc = eA(nc.sbuf_tensor("kv_acc", [128, D], F32))
            wk_sb = eA(nc.sbuf_tensor("wk_sb", [128, NDC * D], BF16))
            wv_sb = eA(nc.sbuf_tensor("wv_sb", [128, NDC * D], BF16))
            xfm = [eA(nc.sbuf_tensor(f"xfmA{s}", [128, NDC * 512], BF16))
                   for s in range(1)]
            k_sb = eA(nc.sbuf_tensor("k_sb", [128, D], BF16))
            v_sb = eA(nc.sbuf_tensor("v_sb", [128, D], BF16))
            dq_pk = [eA(nc.sbuf_tensor(f"dq_pk{s}", [128, 512], I32))
                     for s in range(2)]
            dq_t0 = eA(nc.sbuf_tensor("dq_t0", [128, 512], BF16))
            dq_sge = eA(nc.sbuf_tensor("dq_sge", [128, 512], BF16))
            dq_m = eA(nc.sbuf_tensor("dq_m", [128, 512], BF16))
            dq_ta = eA(nc.sbuf_tensor("dq_ta", [128, 512], BF16))
            dq_tb = eA(nc.sbuf_tensor("dq_tb", [128, 512], BF16))
            dq_val = eA(nc.sbuf_tensor("dq_val", [128, 4096], BF16))
            dq_out = eA(nc.sbuf_tensor("dq_out", [128, 4096], BF16))
            dq_sc = eA(nc.sbuf_tensor("dq_sc", [128, 264], F32))
            dq_tr = [eA(nc.sbuf_tensor(f"dq_tr{s}", [128, GUP_PC], BF16))
                     for s in range(2)]

            for nm, src, dst in (("wk", wkT, wk_sb), ("wv", wvT, wv_sb)):
                src3 = src.rearrange("(c p) e -> p c e", p=128)
                for c in range(4):
                    ctx.dma("pool",
                            lambda E, c=c, src3=src3, dst=dst:
                            E.dma_start(dst[:, c * 4 * D:(c + 1) * 4 * D],
                                        src3[:, c * 4:(c + 1) * 4, :]),
                            writes=[(nm, c)])

            ctx.dma("sp", lambda E: E.dma_start(
                dq_sc[:, 0:176], gup_sc.rearrange("(c p) g -> p c g", p=128)),
                    writes=["dq_sc"])
            ctx.dma("sp", lambda E: E.dma_start(
                dq_sc[:, 176:264], dn_sc.rearrange("(c p) g -> p c g", p=128)),
                    writes=["dq_sc2"])

            dq_nib = eA(nc.sbuf_tensor("dq_nib", [128, 512], I32))

            def decode2():
                jobs = []
                for j in range(6):
                    parts = [(oc, 0, oc * 16, 16, 0)
                             for oc in range(2 * j, min(2 * j + 2, 11))]
                    jobs.append(("g", 256, parts))
                for oc in range(2):
                    for hi in range(2):
                        parts = [(oc, (2 * hi + q) * 176,
                                  176 + oc * 44 + (2 * hi + q) * 11, 11,
                                  (2 * hi + q) * 1408) for q in range(2)]
                        jobs.append(("d", 176, parts))
                for jn, (kind, W, parts) in enumerate(jobs):
                    NP = len(parts)
                    WT = W * NP
                    slot = jn % 2
                    pk_src = gup_pk if kind == "g" else dn_pk
                    pk3 = pk_src.rearrange("(c p) w -> c p w", p=128)
                    kpk = ("dq_pk", slot)

                    for i, (oc, wo, _so, _ng, _co) in enumerate(parts):
                        ctx.dma("sp", lambda E, i=i, oc=oc, wo=wo, W=W,
                                slot=slot, pk3=pk3: E.dma_start(
                                    dq_pk[slot][:, i * W:(i + 1) * W],
                                    pk3[oc, :, wo:wo + W]),
                                writes=[kpk])
                    yield
                    for k in range(8):
                        ctx.op("dve", lambda E, k=k, W=WT, slot=slot:
                               E.tensor_scalar(dq_nib[:, 0:W],
                                               dq_pk[slot][:, 0:W], 4 * k, 15,
                                               ALU.logical_shift_right,
                                               ALU.bitwise_and),
                               reads=[kpk], writes=["nib"])
                        ctx.op("act", lambda E, W=WT: E.activation(
                            dq_t0[:, 0:W], dq_nib[:, 0:W], AF.Copy),
                               reads=["nib"], writes=["t0"])
                        ctx.op("dve", lambda E, W=WT: E.tensor_scalar(
                            dq_sge[:, 0:W], dq_t0[:, 0:W], 8.0, None,
                            ALU.is_ge),
                               reads=["t0"], writes=["sge"])
                        ctx.op("dve", lambda E, W=WT: E.scalar_tensor_tensor(
                            dq_m[:, 0:W], dq_sge[:, 0:W], -8.0,
                            dq_t0[:, 0:W], ALU.mult, ALU.add),
                               reads=["t0", "sge"], writes=["m"])
                        ctx.op("dve", lambda E, W=WT: E.tensor_scalar(
                            dq_ta[:, 0:W], dq_m[:, 0:W], 0.5, None,
                            ALU.mult),
                               reads=["m"], writes=["ta"])
                        ctx.op("dve", lambda E, W=WT: E.scalar_tensor_tensor(
                            dq_ta[:, 0:W], dq_m[:, 0:W], -2.0,
                            dq_ta[:, 0:W], ALU.add, ALU.max),
                               reads=["m", "ta"], writes=["ta"])
                        ctx.op("dve", lambda E, W=WT: E.tensor_scalar(
                            dq_tb[:, 0:W], dq_m[:, 0:W], 2.0, -8.0,
                            ALU.mult, ALU.add),
                               reads=["m"], writes=["tb"])
                        ctx.op("dve", lambda E, W=WT: E.tensor_max(
                            dq_ta[:, 0:W], dq_ta[:, 0:W], dq_tb[:, 0:W]),
                               reads=["ta", "tb"], writes=["ta"])
                        ctx.op("dve", lambda E, W=WT: E.tensor_scalar(
                            dq_tb[:, 0:W], dq_sge[:, 0:W], -2.0, 1.0,
                            ALU.mult, ALU.add),
                               reads=["sge"], writes=["tb"])
                        ctx.op("dve", lambda E, k=k, W=W, NP=NP:
                               E.tensor_mul(
                                   dq_val[:, k:8 * W * NP:8],
                                   dq_ta[:, 0:W * NP], dq_tb[:, 0:W * NP]),
                               reads=["ta", "tb"], writes=["dq_val"])
                        if k % 3 == 2:
                            yield
                    dstt = slice_g if kind == "g" else slice_d
                    for i, (oc, wo, so, NG, co) in enumerate(parts):
                        for g in range(NG):
                            ctx.op("act", lambda E, i=i, g=g, so=so, W=W:
                                   E.activation(
                                       dq_out[:, i * 8 * W + g * 128:
                                              i * 8 * W + (g + 1) * 128],
                                       dq_val[:, i * 8 * W + g * 128:
                                              i * 8 * W + (g + 1) * 128],
                                       AF.Copy,
                                       scale=dq_sc[:, so + g:so + g + 1]),
                                   reads=["dq_val", "dq_sc", "dq_sc2"],
                                   writes=["dq_out"])
                        ctx.dma("sp", lambda E, dstt=dstt, oc=oc, co=co,
                                i=i, W=W: E.dma_start(
                                    dstt[oc * 128:(oc + 1) * 128,
                                         co:co + 8 * W],
                                    dq_out[:, i * 8 * W:(i + 1) * 8 * W]),
                                reads=["dq_out"],
                                writes=[("slice", kind, oc, co)])
                    yield

            dq_gen = decode2()
            dq_done = False

            def pump(n):
                nonlocal dq_done
                for _ in range(n):
                    if dq_done:
                        return
                    try:
                        next(dq_gen)
                    except StopIteration:
                        dq_done = True
                        return

            def proj_group(w_sb, wkeys, slot, lq, eg, bank):
                def f(E, w_sb=w_sb, slot=slot, lq=lq, eg=eg, bank=bank):
                    inst = None
                    for dc in range(NDC):
                        inst = E.matmul(
                            pb[bank][:],
                            xfm[slot][:, dc * 512 + lq * 128:
                                      dc * 512 + (lq + 1) * 128],
                            w_sb[:, dc * D + eg * 512:dc * D + (eg + 1) * 512],
                            start=(dc == 0), stop=(dc == NDC - 1))
                    return inst
                return f

            for lg in range(NDC):
                q, lq = divmod(lg, 4)
                slot = 0
                if lq == 0:
                    for dc in range(NDC):
                        ctx.dma("sp",
                                lambda E, q=q, dc=dc, slot=slot: E.dma_start(
                                    xfm[slot][:, dc * 512:(dc + 1) * 512],
                                    x_bf[q * 512:(q + 1) * 512,
                                         dc * 128:(dc + 1) * 128],
                                    transpose=True),
                                reads=[("x_bf", q)],
                                writes=[("xfmA", slot, dc)])
                xr = [("xfmA", slot, dc) for dc in range(NDC)]
                for eg in range(4):
                    bk, bv = eg % 2, 2 + eg % 2
                    ctx.op("pe", proj_group(wk_sb, None, slot, lq, eg, bk),
                           reads=xr + [("wk", c) for c in range(4)],
                           writes=[PB(bk)])
                    ctx.op("act", lambda E, eg=eg, bk=bk: E.activation(
                        k_sb[:, eg * 512:(eg + 1) * 512], pb[bk][:], AF.Relu,
                        scale=INV_SCALE),
                           reads=[PB(bk)], writes=[("k_sb", eg)])
                    ctx.op("pe", proj_group(wv_sb, None, slot, lq, eg, bv),
                           reads=xr + [("wv", c) for c in range(4)],
                           writes=[PB(bv)])
                    ctx.op("act", lambda E, eg=eg, bv=bv: E.activation(
                        v_sb[:, eg * 512:(eg + 1) * 512], pb[bv][:], AF.Copy),
                           reads=[PB(bv)], writes=[("v_sb", eg)])
                    pump(1)

                def kvmm(E):
                    inst = None
                    for h in range(H):
                        inst = E.matmul(
                            pb[4 + h // 4][:, (h % 4) * 128:(h % 4 + 1) * 128],
                            k_sb[:, h * 128:(h + 1) * 128],
                            v_sb[:, h * 128:(h + 1) * 128],
                            start=True, stop=True)
                    return inst

                ctx.op("pe", kvmm,
                       reads=[("k_sb", e) for e in range(4)]
                       + [("v_sb", e) for e in range(4)],
                       writes=[PB(4), PB(5), PB(6), PB(7)])
                for b in range(4):
                    if lg == 0:
                        ctx.op("dve", lambda E, b=b: E.tensor_copy(
                            kv_acc[:, b * 512:(b + 1) * 512], pb[4 + b][:]),
                               reads=[PB(4 + b)], writes=[("kv_acc", b)])
                    else:
                        ctx.op("dve", lambda E, b=b: E.tensor_add(
                            kv_acc[:, b * 512:(b + 1) * 512],
                            kv_acc[:, b * 512:(b + 1) * 512], pb[4 + b][:]),
                               reads=[PB(4 + b), ("kv_acc", b)],
                               writes=[("kv_acc", b)])

            pump(10000)

            cg3 = contrib_g.rearrange("(j d) o -> j d o", j=11)
            g_reads = [("slice", "g", oc, 0) for oc in range(11)]
            for ic in range(NDC):
                s = ic % 2
                ctx.dma("sp", lambda E, ic=ic, s=s: E.dma_start(
                    dq_tr[s][:, 0:GUP_PC],
                    slice_g[:, ic * 128:(ic + 1) * 128], transpose=True),
                        reads=g_reads, writes=[("dq_tr", s)])
                dst = cg3[:, ic * 128:(ic + 1) * 128, :].rearrange(
                    "j p o -> p j o")
                ctx.dma("sp", lambda E, dst=dst, s=s: E.dma_start(
                    dst,
                    dq_tr[s][:, 0:GUP_PC].rearrange("p (j o) -> p j o", o=128)),
                        reads=[("dq_tr", s)], writes=[("contrib_g", ic)])
            d_reads = [("slice", "d", oc, co) for oc in range(2)
                       for co in (0, 1408, 2816, 4224)]
            for ic in range(NIC):
                s = ic % 2
                ctx.dma("sp", lambda E, ic=ic, s=s: E.dma_start(
                    dq_tr[s][:, 0:DN_PC],
                    slice_d[:, ic * 128:(ic + 1) * 128], transpose=True),
                        reads=d_reads, writes=[("dq_tr", s)])
                ctx.dma("sp", lambda E, ic=ic, s=s: E.dma_start(
                    contrib_d[ic * 128:(ic + 1) * 128, :],
                    dq_tr[s][:, 0:DN_PC]),
                        reads=[("dq_tr", s)], writes=[("contrib_d", ic)])

            ctx.collective(lambda E: E.collective_compute(
                "AllGather", ALU.bypass, replica_groups=RG_ALL,
                ins=[contrib_all.opt()], outs=[gath_all.opt()]),
                reads=[("contrib_g", ic) for ic in range(NDC)]
                + [("contrib_d", ic) for ic in range(NIC)],
                writes=["gath_g", "gath_d"])

            ctx.dma("sp", lambda E: E.dma_start(kv_part, kv_acc[:]),
                    reads=[("kv_acc", b) for b in range(4)],
                    writes=["kv_part"])
            ctx.collective(lambda E: E.collective_compute(
                "AllReduce", ALU.add, replica_groups=RG_PAIR,
                ins=[kv_part.opt()], outs=[kv_red.opt()]),
                reads=["kv_part"], writes=["kv_red"])
            ctx.dma("sp", lambda E: E.dma_start(kv_acc[:], kv_red),
                    reads=["kv_red"], writes=[("kv_acc", "full")])
            if debug:
                ctx.dma("sp", lambda E: E.dma_start(kv_dbg, kv_acc[:]),
                        reads=[("kv_acc", "full")], writes=["kv_dbg"])
            ctx.op("dve", lambda E: E.tensor_copy(kv_bf[:], kv_acc[:]),
                   reads=[("kv_acc", "full")], writes=["kv_bf"])

            ctx.barrier()

        with contextlib.ExitStack() as stB:
            eB = stB.enter_context
            wo_sb = eB(nc.sbuf_tensor("wo_sb", [128, NDC * D], BF16))
            q_sb = eB(nc.sbuf_tensor("q_sb", [128, H * 1024], BF16))
            u_sb = eB(nc.sbuf_tensor("u_sb", [128, H * 1024], BF16))
            obf = [eB(nc.sbuf_tensor(f"obf{p}", [128, 512], BF16))
                   for p in range(2)]
            xfmB = [eB(nc.sbuf_tensor(f"xfmB{s}", [128, NDC * 512], BF16))
                    for s in range(2)]
            wq_blk = [eB(nc.sbuf_tensor(f"wq_blk{s}", [128, D], BF16))
                      for s in range(2)]
            wu_blk = [eB(nc.sbuf_tensor(f"wu_blk{s}", [128, D], BF16))
                      for s in range(2)]
            xres = [eB(nc.sbuf_tensor(f"xres{s}", [128, D], F32))
                    for s in range(2)]
            o2t = [eB(nc.sbuf_tensor(f"o2t{p}", [128, 512], BF16))
                   for p in range(2)]
            srow = [eB(nc.sbuf_tensor(f"srow{p}", [1, 512], F32))
                    for p in range(2)]
            srow_bf = [eB(nc.sbuf_tensor(f"srow_bf{p}", [1, 512], BF16))
                       for p in range(2)]


            wo3 = woT.rearrange("(c p) e -> p c e", p=128)
            for c in range(4):
                ctx.dma("pool",
                        lambda E, c=c: E.dma_start(
                            wo_sb[:, c * 4 * D:(c + 1) * 4 * D],
                            wo3[:, c * 4:(c + 1) * 4, :]),
                        writes=[("wo", c)])

            wq4 = wq_bf.rearrange("(ec dc p) o -> ec p dc o", p=128, dc=NDC)
            wu4 = wu_bf.rearrange("(ec dc p) o -> ec p dc o", p=128, dc=NDC)

            for hf in range(2):
                for ls in range(2):
                    qq = hf * 2 + ls
                    qs = qq % 2
                    for dc in range(NDC):
                        ctx.dma("sp",
                                lambda E, qq=qq, dc=dc, qs=qs: E.dma_start(
                                    xfmB[qs][:, dc * 512:(dc + 1) * 512],
                                    x_bf[qq * 512:(qq + 1) * 512,
                                         dc * 128:(dc + 1) * 128],
                                    transpose=True),
                                reads=[("x_bf", qq)],
                                writes=[("xfmB", qs, dc)])
                    xr = [("xfmB", qs, dc) for dc in range(NDC)]

                    def ldqu(ec):
                        s = ec % 2
                        ctx.dma("act", lambda E, ec=ec, s=s: E.dma_start(
                            wq_blk[s][:], wq4[ec]),
                                writes=[("wq_blk", s)])
                        ctx.dma("act", lambda E, ec=ec, s=s: E.dma_start(
                            wu_blk[s][:], wu4[ec]),
                                writes=[("wu_blk", s)])

                    ldqu(0)
                    for ec in range(NDC):
                        s = ec % 2
                        if ec + 1 < NDC:
                            ldqu(ec + 1)

                        def qmm(E, ec, s, bank, w, qs):
                            inst = None
                            for dc in range(NDC):
                                inst = E.matmul(
                                    pb[bank][:],
                                    w[s][:, dc * 128:(dc + 1) * 128],
                                    xfmB[qs][:, dc * 512:(dc + 1) * 512],
                                    start=(dc == 0), stop=(dc == NDC - 1))
                            return inst

                        ctx.op("pe",
                               lambda E, ec=ec, s=s, qs=qs, f=qmm:
                               f(E, ec, s, ec % 2, wq_blk, qs),
                               reads=xr + [("wq_blk", s)],
                               writes=[PB(ec % 2)])
                        ctx.op("act", lambda E, ec=ec, ls=ls: E.activation(
                            q_sb[:, ec * 1024 + ls * 512:
                                 ec * 1024 + (ls + 1) * 512],
                            pb[ec % 2][:], AF.Relu, scale=INV_SCALE),
                               reads=[PB(ec % 2)], writes=[("q_sb", ec, ls)])
                        ctx.op("pe",
                               lambda E, ec=ec, s=s, qs=qs, f=qmm:
                               f(E, ec, s, 2 + ec % 2, wu_blk, qs),
                               reads=xr + [("wu_blk", s)],
                               writes=[PB(2 + ec % 2)])
                        ctx.op("act", lambda E, ec=ec, ls=ls: E.activation(
                            u_sb[:, ec * 1024 + ls * 512:
                                 ec * 1024 + (ls + 1) * 512],
                            pb[2 + ec % 2][:], AF.Silu),
                               reads=[PB(2 + ec % 2)], writes=[("u_sb", ec, ls)])

                for h in range(H):
                    for ls in range(2):
                        p = (h * 2 + ls) % 2
                        ctx.op("pe", lambda E, h=h, ls=ls, p=p: E.matmul(
                            pb[p][:], kv_bf[:, h * 128:(h + 1) * 128],
                            q_sb[:, h * 1024 + ls * 512:
                                 h * 1024 + (ls + 1) * 512],
                            start=True, stop=True),
                               reads=["kv_bf", ("q_sb", h, ls)],
                               writes=[PB(p)])
                        ctx.op("act", lambda E, p=p: E.activation(
                            obf[p][:], pb[p][:], AF.Copy),
                               reads=[PB(p)], writes=[("obf", p)])
                        ctx.op("act", lambda E, p=p: E.activation(
                            o2t[p][:], pb[p][:], AF.Square),
                               reads=[PB(p)], writes=[("o2t", p)])
                        ctx.op("pe", lambda E, p=p: E.matmul(
                            pb[2 + p][0:1, :], ones_col[:], o2t[p][:],
                            start=True, stop=True),
                               reads=["ones_col", ("o2t", p)],
                               writes=[PB(2 + p)])
                        ctx.op("act", lambda E, p=p: E.activation(
                            srow[p][0:1, :], pb[2 + p][0:1, :], AF.Sqrt,
                            scale=1.0 / HD),
                               reads=[PB(2 + p)], writes=[("srow", p)])
                        ctx.op("dve", lambda E, p=p: E.tensor_scalar(
                            srow[p][0:1, :], srow[p][0:1, :], 1e-12, None,
                            ALU.max),
                               reads=[("srow", p)], writes=[("srow", p)])
                        ctx.op("dve", lambda E, p=p: E.reciprocal(
                            srow[p][0:1, :], srow[p][0:1, :]),
                               reads=[("srow", p)], writes=[("srow", p)])
                        ctx.op("dve", lambda E, p=p: E.tensor_copy(
                            srow_bf[p][0:1, :], srow[p][0:1, :]),
                               reads=[("srow", p)], writes=[("srow_bf", p)])
                        ctx.op("pe", lambda E, p=p: E.matmul(
                            pb[4 + p][:], ones_row[:], srow_bf[p][0:1, :],
                            start=True, stop=True),
                               reads=[("srow_bf", p), "ones_row"],
                               writes=[PB(4 + p)])
                        ctx.op("dve", lambda E, p=p: E.tensor_mul(
                            o2t[p][:], obf[p][:], pb[4 + p][:]),
                               reads=[("obf", p), PB(4 + p)],
                               writes=[("o2t", p)])
                        ctx.op("dve", lambda E, h=h, ls=ls, p=p: E.tensor_mul(
                            q_sb[:, h * 1024 + ls * 512:
                                 h * 1024 + (ls + 1) * 512],
                            o2t[p][:],
                            u_sb[:, h * 1024 + ls * 512:
                                 h * 1024 + (ls + 1) * 512]),
                               reads=[("o2t", p), ("u_sb", h, ls)],
                               writes=[("q_sb", h, ls)])

                for lc in range(8):
                    s = lc % 2
                    row0 = hf * 1024 + lc * 128
                    ctx.dma("sp", lambda E, row0=row0, s=s: E.dma_start(
                        xres[s][:], x_tm[row0:row0 + 128, :]),
                            writes=[("xres", s)])
                    ls = lc // 4
                    for eo in range(4):
                        def womm(E, lc=lc, eo=eo, ls=ls):
                            inst = None
                            for ec in range(NDC):
                                inst = E.matmul(
                                    pb[eo][:],
                                    q_sb[:, ec * 1024 + lc * 128:
                                         ec * 1024 + (lc + 1) * 128],
                                    wo_sb[:, ec * D + eo * 512:
                                          ec * D + (eo + 1) * 512],
                                    start=(ec == 0), stop=(ec == NDC - 1))
                            return inst

                        ctx.op("pe", womm,
                               reads=[("q_sb", ec, ls) for ec in range(NDC)]
                               + [("wo", c) for c in range(4)],
                               writes=[PB(eo)])
                        ctx.op("dve", lambda E, eo=eo, s=s: E.tensor_add(
                            xres[s][:, eo * 512:(eo + 1) * 512],
                            xres[s][:, eo * 512:(eo + 1) * 512], pb[eo][:]),
                               reads=[PB(eo), ("xres", s)],
                               writes=[("xres", s)])
                    ctx.op("dve", lambda E, s=s: E.reduce_sum(
                        colt[:, 0:1], xres[s][:], AX.X, negate=True),
                           reads=[("xres", s)], writes=["nm"])
                    ctx.op("dve", lambda E: E.tensor_scalar(
                        colt[:, 0:1], colt[:, 0:1], 1.0 / D, None, ALU.mult),
                           reads=["nm"], writes=["nm"])
                    ctx.op("dve", lambda E, s=s: E.tensor_scalar(
                        xres[s][:], xres[s][:], colt[:, 0:1], None, ALU.add),
                           reads=["nm", ("xres", s)], writes=[("xres", s)])
                    ctx.op("act", lambda E, s=s: E.activation(
                        xres[1 - s][:], xres[s][:], AF.Square,
                        accum_out=colt[:, 1:2]),
                           reads=[("xres", s)],
                           writes=[("xres", 1 - s), "vacc"])
                    ctx.op("dve", lambda E: E.tensor_scalar(
                        colt[:, 2:3], colt[:, 1:2], 1.0 / D, LN_EPS,
                        ALU.mult, ALU.add),
                           reads=["vacc"], writes=["rstd"])
                    ctx.op("act", lambda E: E.activation(
                        colt[:, 2:3], colt[:, 2:3], AF.Sqrt),
                           reads=["rstd"], writes=["rstd"])
                    ctx.op("dve", lambda E: E.reciprocal(
                        colt[:, 2:3], colt[:, 2:3]),
                           reads=["rstd"], writes=["rstd"])
                    ctx.op("dve", lambda E, s=s: E.tensor_scalar(
                        xres[s][:], xres[s][:], colt[:, 2:3], None, ALU.mult),
                           reads=["rstd", ("xres", s)], writes=[("xres", s)])
                    ctx.dma("sp", lambda E, row0=row0, s=s: E.dma_start(
                        x1_f32[row0:row0 + 128, :], xres[s][:]),
                            reads=[("xres", s)], writes=[("x1f", row0)])
                    ctx.dma("pool", lambda E, row0=row0, s=s: E.dma_start(
                        x1_bf[row0:row0 + 128, :], xres[s][:]),
                            reads=[("xres", s)], writes=[("x1b", row0)])

            ctx.barrier()

        with contextlib.ExitStack() as stC:
            eC = stC.enter_context
            h_sb = eC(nc.sbuf_tensor("h_sb", [128, NIC * 512], BF16))
            x1f_sb = [eC(nc.sbuf_tensor(f"x1f{s}", [128, NDC * 512], BF16))
                      for s in range(2)]
            wg_blk = [eC(nc.sbuf_tensor(f"wg_blk{s}", [128, D], BF16))
                      for s in range(4)]
            wd_sb = [eC(nc.sbuf_tensor(f"wd_sb{s}", [128, NIC * 128], BF16))
                     for s in range(2)]
            y_tm = eC(nc.sbuf_tensor("y_tm", [128, 4 * D], F32))
            sq_scr = eC(nc.sbuf_tensor("sq_scrC", [128, D], F32))
            gt = eC(nc.sbuf_tensor("gt", [128, 512], BF16))
            yf = [eC(nc.sbuf_tensor(f"yf{s}", [128, 512], F32))
                  for s in range(2)]
            x1r = [eC(nc.sbuf_tensor(f"x1r{s}", [128, D], F32))
                   for s in range(2)]
            ob = [eC(nc.sbuf_tensor(f"ob{s}", [128, D], F32))
                  for s in range(2)]

            gg4 = {}
            for oc in range(88):
                r, j = divmod(oc, 11)
                blk = gath_all[r * CROWS + j * D * ... if False else
                               r * CROWS + j * 0:0, :]
            gg4 = [gath_all[(oc // 11) * CROWS + (oc % 11) * 2048:
                            (oc // 11) * CROWS + (oc % 11 + 1) * 2048, :]
                   .rearrange("(dc p) o -> p dc o", p=128)
                   for oc in range(88)]

            def c0(t):
                ts = t % 2
                r0 = t * 512
                for dc in range(NDC):
                    ctx.dma("sp", lambda E, r0=r0, dc=dc, ts=ts: E.dma_start(
                        x1f_sb[ts][:, dc * 512:(dc + 1) * 512],
                        x1_bf[r0:r0 + 512, dc * 128:(dc + 1) * 128],
                        transpose=True),
                            reads=[("x1b", r0), ("x1b", r0 + 128),
                                   ("x1b", r0 + 256), ("x1b", r0 + 384)],
                            writes=[("x1f", ts, dc)])

            c0(0)
            for t in range(4):
                ts = t % 2
                r0 = t * 512
                xr = [("x1f", ts, dc) for dc in range(NDC)]
                def ldg(ic):
                    sg, su = (ic % 2) * 2, (ic % 2) * 2 + 1
                    ctx.dma("act", lambda E, ic=ic, sg=sg: E.dma_start(
                        wg_blk[sg][:], gg4[ic]),
                            reads=["gath_g"], writes=[("wg", sg)])
                    ctx.dma("act", lambda E, ic=ic, su=su: E.dma_start(
                        wg_blk[su][:], gg4[44 + ic]),
                            reads=["gath_g"], writes=[("wg", su)])

                ldg(0)
                ldg(1)
                for ic in range(NIC):
                    sg, su = (ic % 2) * 2, (ic % 2) * 2 + 1

                    def gmm(E, w, bank, ts=ts):
                        inst = None
                        for dc in range(NDC):
                            inst = E.matmul(
                                pb[bank][:], w[:, dc * 128:(dc + 1) * 128],
                                x1f_sb[ts][:, dc * 512:(dc + 1) * 512],
                                start=(dc == 0), stop=(dc == NDC - 1))
                        return inst

                    bkg, bku = ic % 2, 2 + ic % 2
                    ctx.op("pe", lambda E, sg=sg, bkg=bkg, ts=ts:
                           gmm(E, wg_blk[sg], bkg, ts),
                           reads=xr + [("wg", sg)], writes=[PB(bkg)])
                    ctx.op("pe", lambda E, su=su, bku=bku, ts=ts:
                           gmm(E, wg_blk[su], bku, ts),
                           reads=xr + [("wg", su)], writes=[PB(bku)])
                    ctx.op("act", lambda E, bkg=bkg: E.activation(
                        gt[:], pb[bkg][:], AF.Silu),
                           reads=[PB(bkg)], writes=["gt"])
                    ctx.op("dve", lambda E, ic=ic, bku=bku: E.tensor_mul(
                        h_sb[:, ic * 512:(ic + 1) * 512], pb[bku][:], gt[:]),
                           reads=[PB(bku), "gt"], writes=[("h", ic)])
                    if ic + 2 < NIC:
                        ldg(ic + 2)
                if t + 1 < 4:
                    c0(t + 1)
                hr = [("h", ic) for ic in range(NIC)]

                def ldd(oc):
                    r, ch = divmod(oc, 2)
                    s = oc % 2
                    base = r * CROWS + CGR
                    srcd = gath_all[base:base + CDR, :].rearrange(
                        "(i c) o -> i c o", c=2)[:, ch, :].rearrange(
                        "(ic p) o -> p ic o", p=128)
                    ctx.dma("act", lambda E, srcd=srcd, s=s: E.dma_start(
                        wd_sb[s][:], srcd),
                            reads=["gath_d"], writes=[("wd", s)])

                ldd(0)
                for oc in range(NDC):
                    s = oc % 2
                    if oc + 1 < NDC:
                        ldd(oc + 1)

                    def ymm(E, s=s, bank=4 + oc % 2):
                        inst = None
                        for ic in range(NIC):
                            inst = E.matmul(
                                pb[bank][:],
                                wd_sb[s][:, ic * 128:(ic + 1) * 128],
                                h_sb[:, ic * 512:(ic + 1) * 512],
                                start=(ic == 0), stop=(ic == NIC - 1))
                        return inst

                    by = 4 + oc % 2
                    ctx.op("pe", lambda E, s=s, by=by: ymm(E, s, by),
                           reads=hr + [("wd", s)], writes=[PB(by)])
                    ctx.op("act", lambda E, s=s, by=by: E.activation(
                        yf[s][:], pb[by][:], AF.Copy),
                           reads=[PB(by)], writes=[("yf", s)])
                    bt = 6 + oc % 2

                    def ytr(E, s=s, bt=bt):
                        inst = None
                        for qq in range(4):
                            inst = E.transpose(
                                pb[bt][:, qq * 128:(qq + 1) * 128],
                                yf[s][:, qq * 128:(qq + 1) * 128], eye_f[:])
                        return inst

                    ctx.op("pe", ytr, reads=[("yf", s), "eye_f"],
                           writes=[PB(bt)])
                    def yas(E, oc=oc, bt=bt):
                        inst = None
                        for qq in range(4):
                            inst = E.tensor_copy(
                                y_tm[:, qq * D + oc * 128:
                                     qq * D + (oc + 1) * 128],
                                pb[bt][:, qq * 128:(qq + 1) * 128])
                        return inst

                    ctx.op("dve", yas, reads=[PB(bt)], writes=[("y", oc)])
                yr = [("y", oc) for oc in range(NDC)]
                def ldx1(lc, r0=r0):
                    s = lc % 2
                    row0 = r0 + lc * 128
                    ctx.dma("sp", lambda E, row0=row0, s=s: E.dma_start(
                        x1r[s][:], x1_f32[row0:row0 + 128, :]),
                            reads=[("x1f", row0)], writes=[("x1r", s)])

                ldx1(0)
                for lc in range(4):
                    s = lc % 2
                    row0 = r0 + lc * 128
                    if lc + 1 < 4:
                        ldx1(lc + 1)
                    ctx.op("act", lambda E, lc=lc: E.activation(
                        sq_scr[:], y_tm[:, lc * D:(lc + 1) * D], AF.Square,
                        accum_out=colt[:, 3:4]),
                           reads=yr, writes=["sq_scr", "ssacc"])
                    ctx.op("dve", lambda E: E.tensor_scalar(
                        colt[:, 4:5], colt[:, 3:4], 1.0 / D, RMS_EPS,
                        ALU.mult, ALU.add),
                           reads=["ssacc"], writes=["rs1"])
                    ctx.op("act", lambda E: E.activation(
                        colt[:, 4:5], colt[:, 4:5], AF.Sqrt),
                           reads=["rs1"], writes=["rs1"])
                    ctx.op("dve", lambda E: E.reciprocal(
                        colt[:, 4:5], colt[:, 4:5]),
                           reads=["rs1"], writes=["rs1"])
                    ctx.op("dve", lambda E, lc=lc, s=s: E.scalar_tensor_tensor(
                        ob[s][:], y_tm[:, lc * D:(lc + 1) * D], colt[:, 4:5],
                        x1r[s][:], ALU.mult, ALU.add),
                           reads=yr + ["rs1", ("x1r", s)], writes=[("ob", s)])
                    ctx.op("dve", lambda E, s=s: E.reduce_sum(
                        colt[:, 5:6], ob[s][:], AX.X, negate=True),
                           reads=[("ob", s)], writes=["nm2"])
                    ctx.op("dve", lambda E: E.tensor_scalar(
                        colt[:, 5:6], colt[:, 5:6], 1.0 / D, None, ALU.mult),
                           reads=["nm2"], writes=["nm2"])
                    ctx.op("dve", lambda E, s=s: E.tensor_scalar(
                        ob[s][:], ob[s][:], colt[:, 5:6], None, ALU.add),
                           reads=["nm2", ("ob", s)], writes=[("ob", s)])
                    ctx.op("act", lambda E, s=s: E.activation(
                        sq_scr[:], ob[s][:], AF.Square,
                        accum_out=colt[:, 6:7]),
                           reads=[("ob", s)], writes=["sq_scr", "vacc2"])
                    ctx.op("dve", lambda E: E.tensor_scalar(
                        colt[:, 7:8], colt[:, 6:7], 1.0 / D, LN_EPS,
                        ALU.mult, ALU.add),
                           reads=["vacc2"], writes=["rstd2"])
                    ctx.op("act", lambda E: E.activation(
                        colt[:, 7:8], colt[:, 7:8], AF.Sqrt),
                           reads=["rstd2"], writes=["rstd2"])
                    ctx.op("dve", lambda E: E.reciprocal(
                        colt[:, 7:8], colt[:, 7:8]),
                           reads=["rstd2"], writes=["rstd2"])
                    ctx.op("dve", lambda E, s=s: E.tensor_scalar(
                        ob[s][:], ob[s][:], colt[:, 7:8], None, ALU.mult),
                           reads=["rstd2", ("ob", s)], writes=[("ob", s)])
                    ctx.dma("sp", lambda E, row0=row0, s=s: E.dma_start(
                        out[row0:row0 + 128, :], ob[s][:]),
                            reads=[("ob", s)], writes=[("out", row0)])

        ctx.final_waits()
        ctx.replay(e0(nc.Block()))

    return nc


def _prep_inputs(L0_attn_x, L0_attn_wq, L0_attn_wk, L0_attn_wv, L0_attn_wu,
                 L0_attn_wo, L0_ffn_gate_up_packed, L0_ffn_gate_up_scales,
                 L0_ffn_down_packed, L0_ffn_down_scales):
    xf = np.ascontiguousarray(
        np.asarray(L0_attn_x, dtype=np.float32).reshape(NCORES * T, D))
    wkT = np.ascontiguousarray(np.asarray(L0_attn_wk, np.float32).T)
    wvT = np.ascontiguousarray(np.asarray(L0_attn_wv, np.float32).T)
    woT = np.ascontiguousarray(np.asarray(L0_attn_wo, np.float32).T)
    wqT = np.asarray(L0_attn_wq, np.float32).T
    wuT = np.asarray(L0_attn_wu, np.float32).T
    wqT_t = np.ascontiguousarray(
        wqT.reshape(D, NDC, HD).transpose(1, 0, 2).reshape(NDC * D, HD))
    wuT_t = np.ascontiguousarray(
        wuT.reshape(D, NDC, HD).transpose(1, 0, 2).reshape(NDC * D, HD))
    gup_pk = np.asarray(L0_ffn_gate_up_packed, np.int32)
    gup_scT = np.ascontiguousarray(
        np.asarray(L0_ffn_gate_up_scales).T.astype(np.float32))
    dn_pk = np.asarray(L0_ffn_down_packed, np.int32)
    dn_scT = np.ascontiguousarray(
        np.asarray(L0_ffn_down_scales).T.astype(np.float32))

    in_maps = []
    for c in range(NCORES):
        in_maps.append({
            "x_tm": np.ascontiguousarray(xf[c * T:(c + 1) * T]),
            "wkT": wkT, "wvT": wvT, "woT": woT,
            "wqT_t": wqT_t, "wuT_t": wuT_t,
            "gup_pk": np.ascontiguousarray(gup_pk[c * GUP_PC:(c + 1) * GUP_PC]),
            "gup_sc": np.ascontiguousarray(gup_scT[c * GUP_PC:(c + 1) * GUP_PC]),
            "dn_pk": np.ascontiguousarray(dn_pk[c * DN_PC:(c + 1) * DN_PC]),
            "dn_sc": np.ascontiguousarray(dn_scT[c * DN_PC:(c + 1) * DN_PC]),
        })
    return in_maps


def kernel(**inputs):
    in_maps = _prep_inputs(**inputs)
    nc = build_program()
    res = run_bass_kernel_spmd(nc, in_maps, core_ids=list(range(NCORES)))
    outs = [res.results[c]["out"] for c in range(NCORES)]
    full = np.concatenate(outs, axis=0).reshape(4, 4096, D)
    return np.ascontiguousarray(full.astype(np.float32))


# version 26
# speedup vs baseline: 1.0792x; 1.0066x over previous
import math
import contextlib
import numpy as np
import ml_dtypes

import concourse.bass as bass
import concourse.mybir as mybir
from concourse.bass_utils import run_bass_kernel_spmd

AF = mybir.ActivationFunctionType
ALU = mybir.AluOpType
AX = mybir.AxisListType
F32 = mybir.dt.float32
BF16 = mybir.dt.bfloat16
F16 = mybir.dt.float16
I32 = mybir.dt.int32

NCORES = 8
T = 2048
D = 2048
NDC = 16
H = 16
HD = 128
FI = 5632
NIC = 44
GUP_PC = 11264 // NCORES
DN_PC = D // NCORES
INV_SCALE = 1.0 / math.sqrt(HD)
LN_EPS = 1e-5
RMS_EPS = 1e-6
RG_ALL = [list(range(NCORES))]
RG_PAIR = [[0, 1], [2, 3], [4, 5], [6, 7]]


class Ctx:

    ENGS = ("pe", "act", "dve", "pool", "sp")

    def __init__(self, nc, stack):
        self.nc = nc
        self.stream = {e: [] for e in self.ENGS}
        self.sem_op = {e: stack.enter_context(nc.semaphore(f"so_{e}"))
                       for e in self.ENGS}
        self.sem_dma = {e: stack.enter_context(nc.semaphore(f"sd_{e}"))
                        for e in self.ENGS}
        self.sem_cc = [stack.enter_context(nc.semaphore(f"s_cc{i}"))
                       for i in range(4)]
        self.cnt_cc = 0
        self.cnt_op = {e: 0 for e in self.ENGS}
        self.cnt_dma = {e: 0 for e in self.ENGS}
        self.last_w = {}
        self.readers = {}
        self.clock = {}
        self.n_waits = 0

    def _wait(self, cons, dep):
        eng, kind, cnt = dep
        if cnt <= 0:
            return
        if eng == cons and kind == "op" and eng == "pe":
            return
        key = (cons, eng, kind)
        if self.clock.get(key, 0) >= cnt:
            return
        self.clock[key] = cnt
        if kind == "cc":
            for i in range(eng + 1):
                k2 = (cons, ("cc", i))
                if self.clock.get(k2, 0) < 1:
                    self.clock[k2] = 1
                    sem2 = self.sem_cc[i]
                    self.stream[cons].append(
                        lambda E, sem2=sem2: E.wait_ge(sem2, 1))
                    self.n_waits += 1
            return
        sem = self.sem_op[eng] if kind == "op" else self.sem_dma[eng]
        self.stream[cons].append(lambda E, sem=sem, cnt=cnt: E.wait_ge(sem, cnt))
        self.n_waits += 1

    def _edges(self, eng, reads, writes):
        for k in reads:
            d = self.last_w.get(k)
            if d:
                self._wait(eng, d)
        for k in writes:
            d = self.last_w.get(k)
            if d and not (d[0] == eng and d[1] == "op" and eng != "pe"):
                self._wait(eng, d)
            elif d and d[0] == eng and d[1] == "op" and eng == "pe":
                self._wait(eng, d)
            for r in self.readers.get(k, ()):
                if not (r[0] == eng and r[1] == "op" and eng != "pe"):
                    self._wait(eng, r)

    def _commit(self, tag, reads, writes):
        for k in writes:
            self.last_w[k] = tag
            self.readers[k] = []
        for k in reads:
            self.readers.setdefault(k, []).append(tag)

    def op(self, eng, fn, reads=(), writes=()):
        self._edges(eng, reads, writes)
        self.cnt_op[eng] += 1
        cnt = self.cnt_op[eng]
        sem = self.sem_op[eng]
        self.stream[eng].append(lambda E, fn=fn, sem=sem: fn(E).then_inc(sem, 1))
        self._commit((eng, "op", cnt), reads, writes)

    def dma(self, eng, fn, reads=(), writes=()):
        self._edges(eng, reads, writes)
        self.cnt_dma[eng] += 16
        cnt = self.cnt_dma[eng]
        sem = self.sem_dma[eng]
        self.stream[eng].append(lambda E, fn=fn, sem=sem: fn(E).then_inc(sem, 16))
        self._commit((eng, "dma", cnt), reads, writes)

    def collective(self, fn, reads=(), writes=()):
        self._edges("pool", reads, writes)
        idx = self.cnt_cc
        self.cnt_cc += 1
        sem = self.sem_cc[idx]
        self.stream["pool"].append(
            lambda E, fn=fn, sem=sem: fn(E).then_inc(sem))
        self._commit((idx, "cc", 1), reads, writes)

    def barrier(self):
        so = dict(self.cnt_op)
        sd = dict(self.cnt_dma)
        for cons in self.ENGS:
            for prod in self.ENGS:
                if prod != cons:
                    self._wait(cons, (prod, "op", so[prod]))
                self._wait(cons, (prod, "dma", sd[prod]))
        self.last_w = {}
        self.readers = {}

    def final_waits(self):
        for e in self.ENGS:
            if self.cnt_dma[e]:
                sem, cnt = self.sem_dma[e], self.cnt_dma[e]
                self.stream[e].append(
                    lambda E, sem=sem, cnt=cnt: E.wait_ge(sem, cnt))

    def replay(self, block):
        s = self.stream

        @block.tensor
        def _(E):
            for f in s["pe"]:
                f(E)

        @block.scalar
        def _(E):
            for f in s["act"]:
                f(E)

        @block.vector
        def _(E):
            for f in s["dve"]:
                f(E)

        @block.gpsimd
        def _(E):
            for f in s["pool"]:
                f(E)

        @block.sync
        def _(E):
            for f in s["sp"]:
                f(E)


def build_program(debug=False):
    nc = bass.Bass("TRN2", target_bir_lowering=False, num_devices=NCORES)
    dbg = "ExternalOutput" if debug else "Internal"

    x_tm = nc.dram_tensor("x_tm", [T, D], F32, kind="ExternalInput").ap()
    wkT = nc.dram_tensor("wkT", [D, D], F32, kind="ExternalInput").ap()
    wvT = nc.dram_tensor("wvT", [D, D], F32, kind="ExternalInput").ap()
    woT = nc.dram_tensor("woT", [D, D], F32, kind="ExternalInput").ap()
    wqT_t = nc.dram_tensor("wqT_t", [NDC * D, HD], F32, kind="ExternalInput").ap()
    wuT_t = nc.dram_tensor("wuT_t", [NDC * D, HD], F32, kind="ExternalInput").ap()
    gup_pk = nc.dram_tensor("gup_pk", [GUP_PC, 256], I32, kind="ExternalInput").ap()
    gup_sc = nc.dram_tensor("gup_sc", [GUP_PC, 16], F32, kind="ExternalInput").ap()
    dn_pk = nc.dram_tensor("dn_pk", [DN_PC, 704], I32, kind="ExternalInput").ap()
    dn_sc = nc.dram_tensor("dn_sc", [DN_PC, 44], F32, kind="ExternalInput").ap()
    out = nc.dram_tensor("out", [T, D], F32, kind="ExternalOutput").ap()

    x_bf = nc.dram_tensor("x_bf", [T, D], BF16, kind="Internal").ap()
    wq_bf = nc.dram_tensor("wq_bf", [NDC * D, HD], BF16, kind="Internal").ap()
    wu_bf = nc.dram_tensor("wu_bf", [NDC * D, HD], BF16, kind="Internal").ap()
    x1_f32 = nc.dram_tensor("x1_f32", [T, D], F32, kind=dbg).ap()
    x1_bf = nc.dram_tensor("x1_bf", [T, D], BF16, kind="Internal").ap()
    slice_g = nc.dram_tensor("slice_g", [GUP_PC, D], BF16, kind="Internal").ap()
    slice_d = nc.dram_tensor("slice_d", [DN_PC, FI], BF16, kind="Internal").ap()
    CGR = 11 * D
    CDR = FI * 2
    CROWS = CGR + CDR
    contrib_all = nc.dram_tensor("contrib_all", [CROWS, HD], BF16,
                                 kind="Internal").ap()
    contrib_g = contrib_all[0:CGR, :]
    contrib_d = contrib_all[CGR:CROWS, :].rearrange("(i c) o -> i (c o)", c=2)
    gath_all = nc.dram_tensor("gath_all", [NCORES * CROWS, HD], BF16,
                              kind="Internal").ap()
    kv_part = nc.dram_tensor("kv_part", [HD, D], F32, kind="Internal").ap()
    kv_red = nc.dram_tensor("kv_red", [HD, D], F32, kind="Internal").ap()
    kv_dbg = (nc.dram_tensor("kv_dbg", [HD, D], F32, kind="ExternalOutput").ap()
              if debug else None)

    with contextlib.ExitStack() as st0:
        e0 = st0.enter_context
        ctx = Ctx(nc, st0)

        pb = [e0(nc.psum_tensor(f"pb{i}", [128, 512], F32)) for i in range(8)]
        PB = lambda i: ("pb", i)

        eye_i = e0(nc.sbuf_tensor("eye_i", [128, 128], I32))
        eye_f = e0(nc.sbuf_tensor("eye_f", [128, 128], F32))
        ones_col = e0(nc.sbuf_tensor("ones_col", [128, 1], BF16))
        ones_row = e0(nc.sbuf_tensor("ones_row", [1, 128], BF16))
        kv_bf = e0(nc.sbuf_tensor("kv_bf", [128, D], BF16))
        colt = e0(nc.sbuf_tensor("colt", [128, 8], F32))

        ctx.op("pool", lambda E: E.iota(eye_i[:], pattern=[[1, 128]], base=0,
                                        channel_multiplier=-1),
               writes=["eye_i"])
        ctx.op("dve", lambda E: E.tensor_scalar(eye_f[:], eye_i[:], 0, None,
                                                ALU.is_equal),
               reads=["eye_i"], writes=["eye_f"])
        ctx.op("dve", lambda E: E.memset(ones_col[:], 1.0), writes=["ones_col"])
        ctx.op("dve", lambda E: E.memset(ones_row[:], 1.0), writes=["ones_row"])

        for c in range(4):
            ctx.dma("pool",
                    lambda E, c=c: E.dma_start(x_bf[c * 512:(c + 1) * 512, :],
                                               x_tm[c * 512:(c + 1) * 512, :]),
                    reads=[], writes=[("x_bf", c)])
        for c in range(4):
            R = NDC * D // 4
            ctx.dma("pool", lambda E, c=c, R=R: E.dma_start(
                wq_bf[c * R:(c + 1) * R, :], wqT_t[c * R:(c + 1) * R, :]),
                    writes=[("wqbf", c)])
            ctx.dma("pool", lambda E, c=c, R=R: E.dma_start(
                wu_bf[c * R:(c + 1) * R, :], wuT_t[c * R:(c + 1) * R, :]),
                    writes=[("wubf", c)])

        with contextlib.ExitStack() as stA:
            eA = stA.enter_context
            kv_acc = eA(nc.sbuf_tensor("kv_acc", [128, D], F32))
            wk_sb = eA(nc.sbuf_tensor("wk_sb", [128, NDC * D], BF16))
            wv_sb = eA(nc.sbuf_tensor("wv_sb", [128, NDC * D], BF16))
            xfm = [eA(nc.sbuf_tensor(f"xfmA{s}", [128, NDC * 512], BF16))
                   for s in range(1)]
            k_sb = eA(nc.sbuf_tensor("k_sb", [128, D], BF16))
            v_sb = eA(nc.sbuf_tensor("v_sb", [128, D], BF16))
            dq_pk = [eA(nc.sbuf_tensor(f"dq_pk{s}", [128, 512], I32))
                     for s in range(2)]
            dq_t0 = eA(nc.sbuf_tensor("dq_t0", [128, 512], BF16))
            dq_sge = eA(nc.sbuf_tensor("dq_sge", [128, 512], BF16))
            dq_m = eA(nc.sbuf_tensor("dq_m", [128, 512], BF16))
            dq_ta = eA(nc.sbuf_tensor("dq_ta", [128, 512], BF16))
            dq_tb = eA(nc.sbuf_tensor("dq_tb", [128, 512], BF16))
            dq_val = eA(nc.sbuf_tensor("dq_val", [128, 4096], BF16))
            dq_out = eA(nc.sbuf_tensor("dq_out", [128, 4096], BF16))
            dq_sc = eA(nc.sbuf_tensor("dq_sc", [128, 264], F32))
            dq_tr = [eA(nc.sbuf_tensor(f"dq_tr{s}", [128, GUP_PC], BF16))
                     for s in range(2)]

            for nm, src, dst in (("wk", wkT, wk_sb), ("wv", wvT, wv_sb)):
                src3 = src.rearrange("(c p) e -> p c e", p=128)
                for c in range(4):
                    ctx.dma("pool",
                            lambda E, c=c, src3=src3, dst=dst:
                            E.dma_start(dst[:, c * 4 * D:(c + 1) * 4 * D],
                                        src3[:, c * 4:(c + 1) * 4, :]),
                            writes=[(nm, c)])

            ctx.dma("sp", lambda E: E.dma_start(
                dq_sc[:, 0:176], gup_sc.rearrange("(c p) g -> p c g", p=128)),
                    writes=["dq_sc"])
            ctx.dma("sp", lambda E: E.dma_start(
                dq_sc[:, 176:264], dn_sc.rearrange("(c p) g -> p c g", p=128)),
                    writes=["dq_sc2"])

            dq_nib = eA(nc.sbuf_tensor("dq_nib", [128, 512], I32))

            def decode2():
                jobs = []
                for j in range(6):
                    parts = [(oc, 0, oc * 16, 16, 0)
                             for oc in range(2 * j, min(2 * j + 2, 11))]
                    jobs.append(("g", 256, parts))
                for oc in range(2):
                    for hi in range(2):
                        parts = [(oc, (2 * hi + q) * 176,
                                  176 + oc * 44 + (2 * hi + q) * 11, 11,
                                  (2 * hi + q) * 1408) for q in range(2)]
                        jobs.append(("d", 176, parts))
                for jn, (kind, W, parts) in enumerate(jobs):
                    NP = len(parts)
                    WT = W * NP
                    slot = jn % 2
                    pk_src = gup_pk if kind == "g" else dn_pk
                    pk3 = pk_src.rearrange("(c p) w -> c p w", p=128)
                    kpk = ("dq_pk", slot)

                    for i, (oc, wo, _so, _ng, _co) in enumerate(parts):
                        ctx.dma("sp", lambda E, i=i, oc=oc, wo=wo, W=W,
                                slot=slot, pk3=pk3: E.dma_start(
                                    dq_pk[slot][:, i * W:(i + 1) * W],
                                    pk3[oc, :, wo:wo + W]),
                                writes=[kpk])
                    yield
                    for k in range(8):
                        ctx.op("dve", lambda E, k=k, W=WT, slot=slot:
                               E.tensor_scalar(dq_nib[:, 0:W],
                                               dq_pk[slot][:, 0:W], 4 * k, 15,
                                               ALU.logical_shift_right,
                                               ALU.bitwise_and),
                               reads=[kpk], writes=["nib"])
                        ctx.op("act", lambda E, W=WT: E.activation(
                            dq_t0[:, 0:W], dq_nib[:, 0:W], AF.Copy),
                               reads=["nib"], writes=["t0"])
                        ctx.op("dve", lambda E, W=WT: E.tensor_scalar(
                            dq_sge[:, 0:W], dq_t0[:, 0:W], 8.0, None,
                            ALU.is_ge),
                               reads=["t0"], writes=["sge"])
                        ctx.op("dve", lambda E, W=WT: E.scalar_tensor_tensor(
                            dq_m[:, 0:W], dq_sge[:, 0:W], -8.0,
                            dq_t0[:, 0:W], ALU.mult, ALU.add),
                               reads=["t0", "sge"], writes=["m"])
                        ctx.op("dve", lambda E, W=WT: E.tensor_scalar(
                            dq_ta[:, 0:W], dq_m[:, 0:W], 0.5, None,
                            ALU.mult),
                               reads=["m"], writes=["ta"])
                        ctx.op("dve", lambda E, W=WT: E.scalar_tensor_tensor(
                            dq_ta[:, 0:W], dq_m[:, 0:W], -2.0,
                            dq_ta[:, 0:W], ALU.add, ALU.max),
                               reads=["m", "ta"], writes=["ta"])
                        ctx.op("dve", lambda E, W=WT: E.tensor_scalar(
                            dq_tb[:, 0:W], dq_m[:, 0:W], 2.0, -8.0,
                            ALU.mult, ALU.add),
                               reads=["m"], writes=["tb"])
                        ctx.op("dve", lambda E, W=WT: E.tensor_max(
                            dq_ta[:, 0:W], dq_ta[:, 0:W], dq_tb[:, 0:W]),
                               reads=["ta", "tb"], writes=["ta"])
                        ctx.op("dve", lambda E, W=WT: E.tensor_scalar(
                            dq_tb[:, 0:W], dq_sge[:, 0:W], -2.0, 1.0,
                            ALU.mult, ALU.add),
                               reads=["sge"], writes=["tb"])
                        ctx.op("dve", lambda E, k=k, W=W, NP=NP:
                               E.tensor_mul(
                                   dq_val[:, k:8 * W * NP:8],
                                   dq_ta[:, 0:W * NP], dq_tb[:, 0:W * NP]),
                               reads=["ta", "tb"], writes=["dq_val"])
                        if k % 3 == 2:
                            yield
                    dstt = slice_g if kind == "g" else slice_d
                    for i, (oc, wo, so, NG, co) in enumerate(parts):
                        for g in range(NG):
                            ctx.op("act", lambda E, i=i, g=g, so=so, W=W:
                                   E.activation(
                                       dq_out[:, i * 8 * W + g * 128:
                                              i * 8 * W + (g + 1) * 128],
                                       dq_val[:, i * 8 * W + g * 128:
                                              i * 8 * W + (g + 1) * 128],
                                       AF.Copy,
                                       scale=dq_sc[:, so + g:so + g + 1]),
                                   reads=["dq_val", "dq_sc", "dq_sc2"],
                                   writes=["dq_out"])
                        ctx.dma("sp", lambda E, dstt=dstt, oc=oc, co=co,
                                i=i, W=W: E.dma_start(
                                    dstt[oc * 128:(oc + 1) * 128,
                                         co:co + 8 * W],
                                    dq_out[:, i * 8 * W:(i + 1) * 8 * W]),
                                reads=["dq_out"],
                                writes=[("slice", kind, oc, co)])
                    yield

            dq_gen = decode2()
            dq_done = False

            def pump(n):
                nonlocal dq_done
                for _ in range(n):
                    if dq_done:
                        return
                    try:
                        next(dq_gen)
                    except StopIteration:
                        dq_done = True
                        return

            def proj_group(w_sb, wkeys, slot, lq, eg, bank):
                def f(E, w_sb=w_sb, slot=slot, lq=lq, eg=eg, bank=bank):
                    inst = None
                    for dc in range(NDC):
                        inst = E.matmul(
                            pb[bank][:],
                            xfm[slot][:, dc * 512 + lq * 128:
                                      dc * 512 + (lq + 1) * 128],
                            w_sb[:, dc * D + eg * 512:dc * D + (eg + 1) * 512],
                            start=(dc == 0), stop=(dc == NDC - 1))
                    return inst
                return f

            for lg in range(NDC):
                q, lq = divmod(lg, 4)
                slot = 0
                if lq == 0:
                    for dc in range(NDC):
                        ctx.dma("sp",
                                lambda E, q=q, dc=dc, slot=slot: E.dma_start(
                                    xfm[slot][:, dc * 512:(dc + 1) * 512],
                                    x_bf[q * 512:(q + 1) * 512,
                                         dc * 128:(dc + 1) * 128],
                                    transpose=True),
                                reads=[("x_bf", q)],
                                writes=[("xfmA", slot, dc)])
                xr = [("xfmA", slot, dc) for dc in range(NDC)]
                for eg in range(4):
                    bk, bv = eg % 2, 2 + eg % 2
                    ctx.op("pe", proj_group(wk_sb, None, slot, lq, eg, bk),
                           reads=xr + [("wk", c) for c in range(4)],
                           writes=[PB(bk)])
                    ctx.op("act", lambda E, eg=eg, bk=bk: E.activation(
                        k_sb[:, eg * 512:(eg + 1) * 512], pb[bk][:], AF.Relu,
                        scale=INV_SCALE),
                           reads=[PB(bk)], writes=[("k_sb", eg)])
                    ctx.op("pe", proj_group(wv_sb, None, slot, lq, eg, bv),
                           reads=xr + [("wv", c) for c in range(4)],
                           writes=[PB(bv)])
                    ctx.op("act", lambda E, eg=eg, bv=bv: E.activation(
                        v_sb[:, eg * 512:(eg + 1) * 512], pb[bv][:], AF.Copy),
                           reads=[PB(bv)], writes=[("v_sb", eg)])
                    pump(1)

                def kvmm(E):
                    inst = None
                    for h in range(H):
                        inst = E.matmul(
                            pb[4 + h // 4][:, (h % 4) * 128:(h % 4 + 1) * 128],
                            k_sb[:, h * 128:(h + 1) * 128],
                            v_sb[:, h * 128:(h + 1) * 128],
                            start=True, stop=True)
                    return inst

                ctx.op("pe", kvmm,
                       reads=[("k_sb", e) for e in range(4)]
                       + [("v_sb", e) for e in range(4)],
                       writes=[PB(4), PB(5), PB(6), PB(7)])
                for b in range(4):
                    if lg == 0:
                        ctx.op("dve", lambda E, b=b: E.tensor_copy(
                            kv_acc[:, b * 512:(b + 1) * 512], pb[4 + b][:]),
                               reads=[PB(4 + b)], writes=[("kv_acc", b)])
                    else:
                        ctx.op("dve", lambda E, b=b: E.tensor_add(
                            kv_acc[:, b * 512:(b + 1) * 512],
                            kv_acc[:, b * 512:(b + 1) * 512], pb[4 + b][:]),
                               reads=[PB(4 + b), ("kv_acc", b)],
                               writes=[("kv_acc", b)])

            pump(10000)

            cg3 = contrib_g.rearrange("(j d) o -> j d o", j=11)
            g_reads = [("slice", "g", oc, 0) for oc in range(11)]
            for ic in range(NDC):
                s = ic % 2
                ctx.dma("sp", lambda E, ic=ic, s=s: E.dma_start(
                    dq_tr[s][:, 0:GUP_PC],
                    slice_g[:, ic * 128:(ic + 1) * 128], transpose=True),
                        reads=g_reads, writes=[("dq_tr", s)])
                dst = cg3[:, ic * 128:(ic + 1) * 128, :].rearrange(
                    "j p o -> p j o")
                ctx.dma("sp", lambda E, dst=dst, s=s: E.dma_start(
                    dst,
                    dq_tr[s][:, 0:GUP_PC].rearrange("p (j o) -> p j o", o=128)),
                        reads=[("dq_tr", s)], writes=[("contrib_g", ic)])
            d_reads = [("slice", "d", oc, co) for oc in range(2)
                       for co in (0, 1408, 2816, 4224)]
            for ic in range(NIC):
                s = ic % 2
                ctx.dma("sp", lambda E, ic=ic, s=s: E.dma_start(
                    dq_tr[s][:, 0:DN_PC],
                    slice_d[:, ic * 128:(ic + 1) * 128], transpose=True),
                        reads=d_reads, writes=[("dq_tr", s)])
                ctx.dma("sp", lambda E, ic=ic, s=s: E.dma_start(
                    contrib_d[ic * 128:(ic + 1) * 128, :],
                    dq_tr[s][:, 0:DN_PC]),
                        reads=[("dq_tr", s)], writes=[("contrib_d", ic)])

            ctx.collective(lambda E: E.collective_compute(
                "AllGather", ALU.bypass, replica_groups=RG_ALL,
                ins=[contrib_all.opt()], outs=[gath_all.opt()]),
                reads=[("contrib_g", ic) for ic in range(NDC)]
                + [("contrib_d", ic) for ic in range(NIC)],
                writes=["gath_g", "gath_d"])

            ctx.dma("sp", lambda E: E.dma_start(kv_part, kv_acc[:]),
                    reads=[("kv_acc", b) for b in range(4)],
                    writes=["kv_part"])
            ctx.collective(lambda E: E.collective_compute(
                "AllReduce", ALU.add, replica_groups=RG_PAIR,
                ins=[kv_part.opt()], outs=[kv_red.opt()]),
                reads=["kv_part"], writes=["kv_red"])
            ctx.dma("sp", lambda E: E.dma_start(kv_acc[:], kv_red),
                    reads=["kv_red"], writes=[("kv_acc", "full")])
            if debug:
                ctx.dma("sp", lambda E: E.dma_start(kv_dbg, kv_acc[:]),
                        reads=[("kv_acc", "full")], writes=["kv_dbg"])
            ctx.op("dve", lambda E: E.tensor_copy(kv_bf[:], kv_acc[:]),
                   reads=[("kv_acc", "full")], writes=["kv_bf"])

            ctx.barrier()

        with contextlib.ExitStack() as stB:
            eB = stB.enter_context
            wo_sb = eB(nc.sbuf_tensor("wo_sb", [128, NDC * D], BF16))
            q_sb = eB(nc.sbuf_tensor("q_sb", [128, H * 1024], BF16))
            u_sb = eB(nc.sbuf_tensor("u_sb", [128, H * 1024], BF16))
            obf = [eB(nc.sbuf_tensor(f"obf{p}", [128, 512], BF16))
                   for p in range(2)]
            xfmB = [eB(nc.sbuf_tensor(f"xfmB{s}", [128, NDC * 512], BF16))
                    for s in range(2)]
            wq_blk = [eB(nc.sbuf_tensor(f"wq_blk{s}", [128, D], BF16))
                      for s in range(2)]
            wu_blk = [eB(nc.sbuf_tensor(f"wu_blk{s}", [128, D], BF16))
                      for s in range(2)]
            xres = [eB(nc.sbuf_tensor(f"xres{s}", [128, D], F32))
                    for s in range(2)]
            o2t = [eB(nc.sbuf_tensor(f"o2t{p}", [128, 512], BF16))
                   for p in range(2)]
            srow = [eB(nc.sbuf_tensor(f"srow{p}", [1, 512], F32))
                    for p in range(2)]
            srow_bf = [eB(nc.sbuf_tensor(f"srow_bf{p}", [1, 512], BF16))
                       for p in range(2)]


            wo3 = woT.rearrange("(c p) e -> p c e", p=128)
            for c in range(4):
                ctx.dma("pool",
                        lambda E, c=c: E.dma_start(
                            wo_sb[:, c * 4 * D:(c + 1) * 4 * D],
                            wo3[:, c * 4:(c + 1) * 4, :]),
                        writes=[("wo", c)])

            wq4 = wq_bf.rearrange("(ec dc p) o -> ec p dc o", p=128, dc=NDC)
            wu4 = wu_bf.rearrange("(ec dc p) o -> ec p dc o", p=128, dc=NDC)

            for hf in range(2):
                for ls in range(2):
                    qq = hf * 2 + ls
                    qs = qq % 2
                    for dc in range(NDC):
                        ctx.dma("sp",
                                lambda E, qq=qq, dc=dc, qs=qs: E.dma_start(
                                    xfmB[qs][:, dc * 512:(dc + 1) * 512],
                                    x_bf[qq * 512:(qq + 1) * 512,
                                         dc * 128:(dc + 1) * 128],
                                    transpose=True),
                                reads=[("x_bf", qq)],
                                writes=[("xfmB", qs, dc)])
                    xr = [("xfmB", qs, dc) for dc in range(NDC)]

                    def ldqu(ec):
                        s = ec % 2
                        ctx.dma("pool", lambda E, ec=ec, s=s: E.dma_start(
                            wq_blk[s][:], wq4[ec]),
                                writes=[("wq_blk", s)])
                        ctx.dma("pool", lambda E, ec=ec, s=s: E.dma_start(
                            wu_blk[s][:], wu4[ec]),
                                writes=[("wu_blk", s)])

                    ldqu(0)
                    for ec in range(NDC):
                        s = ec % 2
                        if ec + 1 < NDC:
                            ldqu(ec + 1)

                        def qmm(E, ec, s, bank, w, qs):
                            inst = None
                            for dc in range(NDC):
                                inst = E.matmul(
                                    pb[bank][:],
                                    w[s][:, dc * 128:(dc + 1) * 128],
                                    xfmB[qs][:, dc * 512:(dc + 1) * 512],
                                    start=(dc == 0), stop=(dc == NDC - 1))
                            return inst

                        ctx.op("pe",
                               lambda E, ec=ec, s=s, qs=qs, f=qmm:
                               f(E, ec, s, ec % 2, wq_blk, qs),
                               reads=xr + [("wq_blk", s)],
                               writes=[PB(ec % 2)])
                        ctx.op("act", lambda E, ec=ec, ls=ls: E.activation(
                            q_sb[:, ec * 1024 + ls * 512:
                                 ec * 1024 + (ls + 1) * 512],
                            pb[ec % 2][:], AF.Relu, scale=INV_SCALE),
                               reads=[PB(ec % 2)], writes=[("q_sb", ec, ls)])
                        ctx.op("pe",
                               lambda E, ec=ec, s=s, qs=qs, f=qmm:
                               f(E, ec, s, 2 + ec % 2, wu_blk, qs),
                               reads=xr + [("wu_blk", s)],
                               writes=[PB(2 + ec % 2)])
                        ctx.op("act", lambda E, ec=ec, ls=ls: E.activation(
                            u_sb[:, ec * 1024 + ls * 512:
                                 ec * 1024 + (ls + 1) * 512],
                            pb[2 + ec % 2][:], AF.Silu),
                               reads=[PB(2 + ec % 2)], writes=[("u_sb", ec, ls)])

                for h in range(H):
                    for ls in range(2):
                        p = (h * 2 + ls) % 2
                        ctx.op("pe", lambda E, h=h, ls=ls, p=p: E.matmul(
                            pb[p][:], kv_bf[:, h * 128:(h + 1) * 128],
                            q_sb[:, h * 1024 + ls * 512:
                                 h * 1024 + (ls + 1) * 512],
                            start=True, stop=True),
                               reads=["kv_bf", ("q_sb", h, ls)],
                               writes=[PB(p)])
                        ctx.op("act", lambda E, p=p: E.activation(
                            obf[p][:], pb[p][:], AF.Copy),
                               reads=[PB(p)], writes=[("obf", p)])
                        ctx.op("act", lambda E, p=p: E.activation(
                            o2t[p][:], pb[p][:], AF.Square),
                               reads=[PB(p)], writes=[("o2t", p)])
                        ctx.op("pe", lambda E, p=p: E.matmul(
                            pb[2 + p][0:1, :], ones_col[:], o2t[p][:],
                            start=True, stop=True),
                               reads=["ones_col", ("o2t", p)],
                               writes=[PB(2 + p)])
                        ctx.op("act", lambda E, p=p: E.activation(
                            srow[p][0:1, :], pb[2 + p][0:1, :], AF.Sqrt,
                            scale=1.0 / HD),
                               reads=[PB(2 + p)], writes=[("srow", p)])
                        ctx.op("dve", lambda E, p=p: E.tensor_scalar(
                            srow[p][0:1, :], srow[p][0:1, :], 1e-12, None,
                            ALU.max),
                               reads=[("srow", p)], writes=[("srow", p)])
                        ctx.op("dve", lambda E, p=p: E.reciprocal(
                            srow[p][0:1, :], srow[p][0:1, :]),
                               reads=[("srow", p)], writes=[("srow", p)])
                        ctx.op("dve", lambda E, p=p: E.tensor_copy(
                            srow_bf[p][0:1, :], srow[p][0:1, :]),
                               reads=[("srow", p)], writes=[("srow_bf", p)])
                        ctx.op("pe", lambda E, p=p: E.matmul(
                            pb[4 + p][:], ones_row[:], srow_bf[p][0:1, :],
                            start=True, stop=True),
                               reads=[("srow_bf", p), "ones_row"],
                               writes=[PB(4 + p)])
                        ctx.op("dve", lambda E, p=p: E.tensor_mul(
                            o2t[p][:], obf[p][:], pb[4 + p][:]),
                               reads=[("obf", p), PB(4 + p)],
                               writes=[("o2t", p)])
                        ctx.op("dve", lambda E, h=h, ls=ls, p=p: E.tensor_mul(
                            q_sb[:, h * 1024 + ls * 512:
                                 h * 1024 + (ls + 1) * 512],
                            o2t[p][:],
                            u_sb[:, h * 1024 + ls * 512:
                                 h * 1024 + (ls + 1) * 512]),
                               reads=[("o2t", p), ("u_sb", h, ls)],
                               writes=[("q_sb", h, ls)])

                for lc in range(8):
                    s = lc % 2
                    row0 = hf * 1024 + lc * 128
                    ctx.dma("sp", lambda E, row0=row0, s=s: E.dma_start(
                        xres[s][:], x_tm[row0:row0 + 128, :]),
                            writes=[("xres", s)])
                    ls = lc // 4
                    for eo in range(4):
                        def womm(E, lc=lc, eo=eo, ls=ls):
                            inst = None
                            for ec in range(NDC):
                                inst = E.matmul(
                                    pb[eo][:],
                                    q_sb[:, ec * 1024 + lc * 128:
                                         ec * 1024 + (lc + 1) * 128],
                                    wo_sb[:, ec * D + eo * 512:
                                          ec * D + (eo + 1) * 512],
                                    start=(ec == 0), stop=(ec == NDC - 1))
                            return inst

                        ctx.op("pe", womm,
                               reads=[("q_sb", ec, ls) for ec in range(NDC)]
                               + [("wo", c) for c in range(4)],
                               writes=[PB(eo)])
                        ctx.op("dve", lambda E, eo=eo, s=s: E.tensor_add(
                            xres[s][:, eo * 512:(eo + 1) * 512],
                            xres[s][:, eo * 512:(eo + 1) * 512], pb[eo][:]),
                               reads=[PB(eo), ("xres", s)],
                               writes=[("xres", s)])
                    ctx.op("dve", lambda E, s=s: E.reduce_sum(
                        colt[:, 0:1], xres[s][:], AX.X, negate=True),
                           reads=[("xres", s)], writes=["nm"])
                    ctx.op("dve", lambda E: E.tensor_scalar(
                        colt[:, 0:1], colt[:, 0:1], 1.0 / D, None, ALU.mult),
                           reads=["nm"], writes=["nm"])
                    ctx.op("dve", lambda E, s=s: E.tensor_scalar(
                        xres[s][:], xres[s][:], colt[:, 0:1], None, ALU.add),
                           reads=["nm", ("xres", s)], writes=[("xres", s)])
                    ctx.op("act", lambda E, s=s: E.activation(
                        xres[1 - s][:], xres[s][:], AF.Square,
                        accum_out=colt[:, 1:2]),
                           reads=[("xres", s)],
                           writes=[("xres", 1 - s), "vacc"])
                    ctx.op("dve", lambda E: E.tensor_scalar(
                        colt[:, 2:3], colt[:, 1:2], 1.0 / D, LN_EPS,
                        ALU.mult, ALU.add),
                           reads=["vacc"], writes=["rstd"])
                    ctx.op("act", lambda E: E.activation(
                        colt[:, 2:3], colt[:, 2:3], AF.Sqrt),
                           reads=["rstd"], writes=["rstd"])
                    ctx.op("dve", lambda E: E.reciprocal(
                        colt[:, 2:3], colt[:, 2:3]),
                           reads=["rstd"], writes=["rstd"])
                    ctx.op("dve", lambda E, s=s: E.tensor_scalar(
                        xres[s][:], xres[s][:], colt[:, 2:3], None, ALU.mult),
                           reads=["rstd", ("xres", s)], writes=[("xres", s)])
                    ctx.dma("sp", lambda E, row0=row0, s=s: E.dma_start(
                        x1_f32[row0:row0 + 128, :], xres[s][:]),
                            reads=[("xres", s)], writes=[("x1f", row0)])
                    ctx.dma("pool", lambda E, row0=row0, s=s: E.dma_start(
                        x1_bf[row0:row0 + 128, :], xres[s][:]),
                            reads=[("xres", s)], writes=[("x1b", row0)])

            ctx.barrier()

        with contextlib.ExitStack() as stC:
            eC = stC.enter_context
            h_sb = eC(nc.sbuf_tensor("h_sb", [128, NIC * 512], BF16))
            x1f_sb = [eC(nc.sbuf_tensor(f"x1f{s}", [128, NDC * 512], BF16))
                      for s in range(2)]
            wg_blk = [eC(nc.sbuf_tensor(f"wg_blk{s}", [128, D], BF16))
                      for s in range(4)]
            wd_sb = [eC(nc.sbuf_tensor(f"wd_sb{s}", [128, NIC * 128], BF16))
                     for s in range(2)]
            y_tm = eC(nc.sbuf_tensor("y_tm", [128, 4 * D], F32))
            sq_scr = eC(nc.sbuf_tensor("sq_scrC", [128, D], F32))
            gt = eC(nc.sbuf_tensor("gt", [128, 512], BF16))
            yf = [eC(nc.sbuf_tensor(f"yf{s}", [128, 512], F32))
                  for s in range(2)]
            x1r = [eC(nc.sbuf_tensor(f"x1r{s}", [128, D], F32))
                   for s in range(2)]
            ob = [eC(nc.sbuf_tensor(f"ob{s}", [128, D], F32))
                  for s in range(2)]

            gg4 = {}
            for oc in range(88):
                r, j = divmod(oc, 11)
                blk = gath_all[r * CROWS + j * D * ... if False else
                               r * CROWS + j * 0:0, :]
            gg4 = [gath_all[(oc // 11) * CROWS + (oc % 11) * 2048:
                            (oc // 11) * CROWS + (oc % 11 + 1) * 2048, :]
                   .rearrange("(dc p) o -> p dc o", p=128)
                   for oc in range(88)]

            def c0(t):
                ts = t % 2
                r0 = t * 512
                for dc in range(NDC):
                    ctx.dma("sp", lambda E, r0=r0, dc=dc, ts=ts: E.dma_start(
                        x1f_sb[ts][:, dc * 512:(dc + 1) * 512],
                        x1_bf[r0:r0 + 512, dc * 128:(dc + 1) * 128],
                        transpose=True),
                            reads=[("x1b", r0), ("x1b", r0 + 128),
                                   ("x1b", r0 + 256), ("x1b", r0 + 384)],
                            writes=[("x1f", ts, dc)])

            c0(0)
            for t in range(4):
                ts = t % 2
                r0 = t * 512
                xr = [("x1f", ts, dc) for dc in range(NDC)]
                def ldg(ic):
                    sg, su = (ic % 2) * 2, (ic % 2) * 2 + 1
                    ctx.dma("act", lambda E, ic=ic, sg=sg: E.dma_start(
                        wg_blk[sg][:], gg4[ic]),
                            reads=["gath_g"], writes=[("wg", sg)])
                    ctx.dma("act", lambda E, ic=ic, su=su: E.dma_start(
                        wg_blk[su][:], gg4[44 + ic]),
                            reads=["gath_g"], writes=[("wg", su)])

                ldg(0)
                ldg(1)
                for ic in range(NIC):
                    sg, su = (ic % 2) * 2, (ic % 2) * 2 + 1

                    def gmm(E, w, bank, ts=ts):
                        inst = None
                        for dc in range(NDC):
                            inst = E.matmul(
                                pb[bank][:], w[:, dc * 128:(dc + 1) * 128],
                                x1f_sb[ts][:, dc * 512:(dc + 1) * 512],
                                start=(dc == 0), stop=(dc == NDC - 1))
                        return inst

                    bkg, bku = ic % 2, 2 + ic % 2
                    ctx.op("pe", lambda E, sg=sg, bkg=bkg, ts=ts:
                           gmm(E, wg_blk[sg], bkg, ts),
                           reads=xr + [("wg", sg)], writes=[PB(bkg)])
                    ctx.op("pe", lambda E, su=su, bku=bku, ts=ts:
                           gmm(E, wg_blk[su], bku, ts),
                           reads=xr + [("wg", su)], writes=[PB(bku)])
                    ctx.op("act", lambda E, bkg=bkg: E.activation(
                        gt[:], pb[bkg][:], AF.Silu),
                           reads=[PB(bkg)], writes=["gt"])
                    ctx.op("dve", lambda E, ic=ic, bku=bku: E.tensor_mul(
                        h_sb[:, ic * 512:(ic + 1) * 512], pb[bku][:], gt[:]),
                           reads=[PB(bku), "gt"], writes=[("h", ic)])
                    if ic + 2 < NIC:
                        ldg(ic + 2)
                if t + 1 < 4:
                    c0(t + 1)
                hr = [("h", ic) for ic in range(NIC)]

                def ldd(oc):
                    r, ch = divmod(oc, 2)
                    s = oc % 2
                    base = r * CROWS + CGR
                    srcd = gath_all[base:base + CDR, :].rearrange(
                        "(i c) o -> i c o", c=2)[:, ch, :].rearrange(
                        "(ic p) o -> p ic o", p=128)
                    ctx.dma("act", lambda E, srcd=srcd, s=s: E.dma_start(
                        wd_sb[s][:], srcd),
                            reads=["gath_d"], writes=[("wd", s)])

                ldd(0)
                for oc in range(NDC):
                    s = oc % 2
                    if oc + 1 < NDC:
                        ldd(oc + 1)

                    def ymm(E, s=s, bank=4 + oc % 2):
                        inst = None
                        for ic in range(NIC):
                            inst = E.matmul(
                                pb[bank][:],
                                wd_sb[s][:, ic * 128:(ic + 1) * 128],
                                h_sb[:, ic * 512:(ic + 1) * 512],
                                start=(ic == 0), stop=(ic == NIC - 1))
                        return inst

                    by = 4 + oc % 2
                    ctx.op("pe", lambda E, s=s, by=by: ymm(E, s, by),
                           reads=hr + [("wd", s)], writes=[PB(by)])
                    ctx.op("act", lambda E, s=s, by=by: E.activation(
                        yf[s][:], pb[by][:], AF.Copy),
                           reads=[PB(by)], writes=[("yf", s)])
                    bt = 6 + oc % 2

                    def ytr(E, s=s, bt=bt):
                        inst = None
                        for qq in range(4):
                            inst = E.transpose(
                                pb[bt][:, qq * 128:(qq + 1) * 128],
                                yf[s][:, qq * 128:(qq + 1) * 128], eye_f[:])
                        return inst

                    ctx.op("pe", ytr, reads=[("yf", s), "eye_f"],
                           writes=[PB(bt)])
                    def yas(E, oc=oc, bt=bt):
                        inst = None
                        for qq in range(4):
                            inst = E.tensor_copy(
                                y_tm[:, qq * D + oc * 128:
                                     qq * D + (oc + 1) * 128],
                                pb[bt][:, qq * 128:(qq + 1) * 128])
                        return inst

                    ctx.op("dve", yas, reads=[PB(bt)], writes=[("y", oc)])
                yr = [("y", oc) for oc in range(NDC)]
                def ldx1(lc, r0=r0):
                    s = lc % 2
                    row0 = r0 + lc * 128
                    ctx.dma("sp", lambda E, row0=row0, s=s: E.dma_start(
                        x1r[s][:], x1_f32[row0:row0 + 128, :]),
                            reads=[("x1f", row0)], writes=[("x1r", s)])

                ldx1(0)
                for lc in range(4):
                    s = lc % 2
                    row0 = r0 + lc * 128
                    if lc + 1 < 4:
                        ldx1(lc + 1)
                    ctx.op("act", lambda E, lc=lc: E.activation(
                        sq_scr[:], y_tm[:, lc * D:(lc + 1) * D], AF.Square,
                        accum_out=colt[:, 3:4]),
                           reads=yr, writes=["sq_scr", "ssacc"])
                    ctx.op("dve", lambda E: E.tensor_scalar(
                        colt[:, 4:5], colt[:, 3:4], 1.0 / D, RMS_EPS,
                        ALU.mult, ALU.add),
                           reads=["ssacc"], writes=["rs1"])
                    ctx.op("act", lambda E: E.activation(
                        colt[:, 4:5], colt[:, 4:5], AF.Sqrt),
                           reads=["rs1"], writes=["rs1"])
                    ctx.op("dve", lambda E: E.reciprocal(
                        colt[:, 4:5], colt[:, 4:5]),
                           reads=["rs1"], writes=["rs1"])
                    ctx.op("dve", lambda E, lc=lc, s=s: E.scalar_tensor_tensor(
                        ob[s][:], y_tm[:, lc * D:(lc + 1) * D], colt[:, 4:5],
                        x1r[s][:], ALU.mult, ALU.add),
                           reads=yr + ["rs1", ("x1r", s)], writes=[("ob", s)])
                    ctx.op("dve", lambda E, s=s: E.reduce_sum(
                        colt[:, 5:6], ob[s][:], AX.X, negate=True),
                           reads=[("ob", s)], writes=["nm2"])
                    ctx.op("dve", lambda E: E.tensor_scalar(
                        colt[:, 5:6], colt[:, 5:6], 1.0 / D, None, ALU.mult),
                           reads=["nm2"], writes=["nm2"])
                    ctx.op("dve", lambda E, s=s: E.tensor_scalar(
                        ob[s][:], ob[s][:], colt[:, 5:6], None, ALU.add),
                           reads=["nm2", ("ob", s)], writes=[("ob", s)])
                    ctx.op("act", lambda E, s=s: E.activation(
                        sq_scr[:], ob[s][:], AF.Square,
                        accum_out=colt[:, 6:7]),
                           reads=[("ob", s)], writes=["sq_scr", "vacc2"])
                    ctx.op("dve", lambda E: E.tensor_scalar(
                        colt[:, 7:8], colt[:, 6:7], 1.0 / D, LN_EPS,
                        ALU.mult, ALU.add),
                           reads=["vacc2"], writes=["rstd2"])
                    ctx.op("act", lambda E: E.activation(
                        colt[:, 7:8], colt[:, 7:8], AF.Sqrt),
                           reads=["rstd2"], writes=["rstd2"])
                    ctx.op("dve", lambda E: E.reciprocal(
                        colt[:, 7:8], colt[:, 7:8]),
                           reads=["rstd2"], writes=["rstd2"])
                    ctx.op("dve", lambda E, s=s: E.tensor_scalar(
                        ob[s][:], ob[s][:], colt[:, 7:8], None, ALU.mult),
                           reads=["rstd2", ("ob", s)], writes=[("ob", s)])
                    ctx.dma("sp", lambda E, row0=row0, s=s: E.dma_start(
                        out[row0:row0 + 128, :], ob[s][:]),
                            reads=[("ob", s)], writes=[("out", row0)])

        ctx.final_waits()
        ctx.replay(e0(nc.Block()))

    return nc


def _prep_inputs(L0_attn_x, L0_attn_wq, L0_attn_wk, L0_attn_wv, L0_attn_wu,
                 L0_attn_wo, L0_ffn_gate_up_packed, L0_ffn_gate_up_scales,
                 L0_ffn_down_packed, L0_ffn_down_scales):
    xf = np.ascontiguousarray(
        np.asarray(L0_attn_x, dtype=np.float32).reshape(NCORES * T, D))
    wkT = np.ascontiguousarray(np.asarray(L0_attn_wk, np.float32).T)
    wvT = np.ascontiguousarray(np.asarray(L0_attn_wv, np.float32).T)
    woT = np.ascontiguousarray(np.asarray(L0_attn_wo, np.float32).T)
    wqT = np.asarray(L0_attn_wq, np.float32).T
    wuT = np.asarray(L0_attn_wu, np.float32).T
    wqT_t = np.ascontiguousarray(
        wqT.reshape(D, NDC, HD).transpose(1, 0, 2).reshape(NDC * D, HD))
    wuT_t = np.ascontiguousarray(
        wuT.reshape(D, NDC, HD).transpose(1, 0, 2).reshape(NDC * D, HD))
    gup_pk = np.asarray(L0_ffn_gate_up_packed, np.int32)
    gup_scT = np.ascontiguousarray(
        np.asarray(L0_ffn_gate_up_scales).T.astype(np.float32))
    dn_pk = np.asarray(L0_ffn_down_packed, np.int32)
    dn_scT = np.ascontiguousarray(
        np.asarray(L0_ffn_down_scales).T.astype(np.float32))

    in_maps = []
    for c in range(NCORES):
        in_maps.append({
            "x_tm": np.ascontiguousarray(xf[c * T:(c + 1) * T]),
            "wkT": wkT, "wvT": wvT, "woT": woT,
            "wqT_t": wqT_t, "wuT_t": wuT_t,
            "gup_pk": np.ascontiguousarray(gup_pk[c * GUP_PC:(c + 1) * GUP_PC]),
            "gup_sc": np.ascontiguousarray(gup_scT[c * GUP_PC:(c + 1) * GUP_PC]),
            "dn_pk": np.ascontiguousarray(dn_pk[c * DN_PC:(c + 1) * DN_PC]),
            "dn_sc": np.ascontiguousarray(dn_scT[c * DN_PC:(c + 1) * DN_PC]),
        })
    return in_maps


def kernel(**inputs):
    in_maps = _prep_inputs(**inputs)
    nc = build_program()
    res = run_bass_kernel_spmd(nc, in_maps, core_ids=list(range(NCORES)))
    outs = [res.results[c]["out"] for c in range(NCORES)]
    full = np.concatenate(outs, axis=0).reshape(4, 4096, D)
    return np.ascontiguousarray(full.astype(np.float32))
